# version 26
# baseline (speedup 1.0000x reference)
import math
import numpy as np
from contextlib import ExitStack

import concourse.bass as bass
import concourse.mybir as mybir
from concourse.bass_utils import run_bass_kernel_spmd

F32 = mybir.dt.float32
BF16 = mybir.dt.bfloat16
AF = mybir.ActivationFunctionType
ALU = mybir.AluOpType
AX = mybir.AxisListType

NT = 1024
D = 2048
DC = 512
DCKV = 512
H = 16
DHD = 128
DRH = 64
HID = 8192
NBLK = NT // 8
SCALER = 1.0 / math.sqrt(DHD + DRH)
EPS = 1e-5
SEQ = 2048
NCORE = 8

NCONST = 106
C_EPS, C_ONE = 0, 1
C_BC, C_BCKV, C_BKR, C_BKRSW = 2, 6, 10, 18
C_BCQ, C_BQR, C_BQRSW = 26, 42, 50
C_BCK, C_BV, C_BO = 58, 74, 90


FM = ("freemem",)


class SbufAlloc:
    def __init__(self, nc, lo=18432, hi=229312):
        self.nc = nc
        self.free = [(lo, hi)]
        self.live = {}

    def alloc(self, name, shape, dtype, align=64):
        per = int(np.prod(shape[1:])) * mybir.dt.size(dtype)
        if per >= 16384:
            for i in range(len(self.free) - 1, -1, -1):
                a, b = self.free[i]
                off = (b - per) // align * align
                if off >= a:
                    if off == a:
                        self.free.pop(i)
                    else:
                        self.free[i] = (a, off)
                    if off + per < b:
                        self.free.insert(i + 1, (off + per, b))
                    self.live[name] = (off, per)
                    return self.nc.alloc_sbuf_tensor_at(name, list(shape), dtype, offset=off)
        else:
            for i, (a, b) in enumerate(self.free):
                off = (a + align - 1) // align * align
                if off + per <= b:
                    if off + per == b:
                        if off == a:
                            self.free.pop(i)
                        else:
                            self.free[i] = (a, off)
                    else:
                        self.free[i] = (off + per, b)
                        if off > a:
                            self.free.insert(i, (a, off))
                    self.live[name] = (off, per)
                    return self.nc.alloc_sbuf_tensor_at(name, list(shape), dtype, offset=off)
        raise MemoryError(f"SBUF alloc failed for {name} ({per}B); free={self.free}")

    def free_(self, name):
        off, per = self.live.pop(name)
        self.free.append((off, off + per))
        self.free.sort()
        merged = []
        for a, b in self.free:
            if merged and merged[-1][1] >= a:
                merged[-1][1] = max(merged[-1][1], b)
            else:
                merged.append([a, b])
        self.free = [tuple(x) for x in merged]


class Prog:

    NLANE = 4

    def __init__(self, nc):
        self.nc = nc
        self.ops = {e: [] for e in "tvsyp"}
        self.sem_names = (["t", "v", "s", "y", "p"] +
                          [f"Y{i}" for i in range(self.NLANE)] +
                          [f"P{i}" for i in range(self.NLANE)])
        self.counts = {sn: 0 for sn in self.sem_names}
        self.dma_idx = {"y": 0, "p": 0}
        self.waited = {e: {} for e in "tvsyp"}
        self.res = {}
        self.stack = ExitStack()
        self.sems = {}

    def open_sems(self):
        for sn in self.sem_names:
            self.sems[sn] = self.stack.enter_context(self.nc.semaphore("sem_" + sn))

    def _dep(self, deps, sem, val):
        if val > deps.get(sem, 0):
            deps[sem] = val

    def emit(self, eng, fns, reads=(), writes=(), dma=False):
        if callable(fns):
            fns = [fns]
        if dma:
            idx = self.dma_idx[eng]
            self.dma_idx[eng] = idx + 1
            lane = idx % self.NLANE
            sem = f"{eng.upper()}{lane}"
            amt = 16
            prior = self.counts[sem]
            if prior > 0 and self.waited[eng].get(sem, 0) < prior:
                self.ops[eng].append(("w", sem, prior))
                self.waited[eng][sem] = prior
        else:
            sem = eng
            amt = 1
        deps = {}
        for k in reads:
            r = self.res.get(k)
            if r and r["w"]:
                self._dep(deps, *r["w"])
            if r and k == FM:
                for s_, v_ in r["r"].items():
                    self._dep(deps, s_, v_)
        for k in writes:
            r = self.res.get(k)
            if r:
                if r["w"]:
                    self._dep(deps, *r["w"])
                for s_, v_ in r["r"].items():
                    self._dep(deps, s_, v_)
        for dsem, dval in sorted(deps.items()):
            if dsem == "t" and eng == "t" and not dma:
                continue
            if self.waited[eng].get(dsem, 0) >= dval:
                continue
            self.ops[eng].append(("w", dsem, dval))
            self.waited[eng][dsem] = dval
        for i, fn in enumerate(fns):
            self.ops[eng].append(("o", fn, sem if i == len(fns) - 1 else None, amt))
        self.counts[sem] += amt
        v = self.counts[sem]
        for k in writes:
            self.res[k] = {"w": (sem, v), "r": {}}
        for k in reads:
            if k == FM:
                continue
            r = self.res.setdefault(k, {"w": None, "r": {}})
            if v > r["r"].get(sem, 0):
                r["r"][sem] = v

    def merge_into(self, dst, prefixes):
        agg = self.res.setdefault(dst, {"w": None, "r": {}})
        for k in list(self.res.keys()):
            if k == dst:
                continue
            tag = k[0] if isinstance(k, tuple) else k
            if tag in prefixes:
                r = self.res.pop(k)
                ents = list(r["r"].items())
                if r["w"]:
                    ents.append(r["w"])
                for s_, v_ in ents:
                    if v_ > agg["r"].get(s_, 0):
                        agg["r"][s_] = v_

    def replay(self):
        block = self.stack.enter_context(self.nc.Block())
        prog = self

        def runner(eng_name):
            def run(engobj):
                for item in prog.ops[eng_name]:
                    if item[0] == "w":
                        engobj.wait_ge(prog.sems[item[1]], item[2])
                    else:
                        inst = item[1](engobj)
                        if item[2] is not None:
                            inst.then_inc(prog.sems[item[2]], item[3])
            return run

        block.tensor(runner("t"))
        block.vector(runner("v"))
        block.scalar(runner("s"))
        block.sync(runner("y"))
        block.gpsimd(runner("p"))


def build_program(taps=()):
    nc = bass.Bass("TRN2", target_bir_lowering=False)
    hb = {}
    hb["xT"] = nc.dram_tensor("xT", [D, NT], F32, kind="ExternalInput").ap()
    hb["consts"] = nc.dram_tensor("consts", [128, NCONST], F32, kind="ExternalInput").ap()
    hb["maskuv"] = nc.dram_tensor("maskuv", [9, 256], F32, kind="ExternalInput").ap()
    hb["ident"] = nc.dram_tensor("ident", [128, 128], F32, kind="ExternalInput").ap()
    hb["onesrow"] = nc.dram_tensor("onesrow", [1, 128], F32, kind="ExternalInput").ap()
    hb["ropeC"] = nc.dram_tensor("ropeC", [128, NT], F32, kind="ExternalInput").ap()
    hb["ropeS2"] = nc.dram_tensor("ropeS2", [128, NT], F32, kind="ExternalInput").ap()
    hb["WB"] = nc.dram_tensor("WB", [24, 128, 16 * 128], F32, kind="ExternalInput").ap()
    hb["WC"] = nc.dram_tensor("WC", [8, 128, 4 * 512], F32, kind="ExternalInput").ap()
    hb["WD"] = nc.dram_tensor("WD", [8, 128, 4 * 512], F32, kind="ExternalInput").ap()
    hb["WO"] = nc.dram_tensor("WO", [16, 128, 16 * 128], F32, kind="ExternalInput").ap()
    hb["WGU"] = nc.dram_tensor("WGU", [64, 128, 2 * 16 * 128], F32, kind="ExternalInput").ap()
    hb["WDN"] = nc.dram_tensor("WDN", [16, 128, 64 * 128], F32, kind="ExternalInput").ap()
    hb["x2s"] = nc.dram_tensor("x2s", [D, NT], F32, kind="Internal").ap()
    hb["outT"] = nc.dram_tensor("outT", [D, NT], F32, kind="ExternalOutput").ap()

    P = Prog(nc)
    P.open_sems()
    SB = SbufAlloc(nc)
    PS = P.stack.enter_context(nc.psum_tensor("PS", [128, 4096], F32))
    psa = PS.ap()

    def bank(i, n=1, dtype=F32):
        ap = psa[:, i * 512:(i + n) * 512]
        return ap if dtype == F32 else ap.bitcast(dtype)

    def bkeys(i, n=1):
        return [("PB", i + j) for j in range(n)]

    tap_keys = []

    def tap(name, sbuf_ap, shape, dtype, reads):
        if name not in taps:
            return
        dt_ = nc.dram_tensor("dbg_" + name, list(shape), dtype, kind="ExternalOutput").ap()
        P.emit("y", lambda e, d=dt_, s=sbuf_ap: e.dma_start(out=d, in_=s),
               reads=list(reads), writes=[("tapk", name)], dma=True)
        tap_keys.append(("tapk", name))

    consts = SB.alloc("consts", [128, NCONST], F32)
    maskuv = SB.alloc("maskuv", [9, 256], BF16)
    ident = SB.alloc("ident", [128, 128], BF16)
    onesrow = SB.alloc("onesrow", [1, 128], F32)
    onesbf = SB.alloc("onesbf", [128, 1], BF16)
    ropeC = SB.alloc("ropeC", [128, NT], F32)
    ropeS2 = SB.alloc("ropeS2", [128, NT], F32)

    def ccol(i):
        return consts.ap()[:, i:i + 1]

    P.emit("y", lambda e: e.dma_start(out=consts.ap(), in_=hb["consts"]), writes=[("consts",)], dma=True)
    P.emit("p", lambda e: e.dma_start(out=maskuv.ap(), in_=hb["maskuv"]), writes=[("maskuv",)], dma=True)
    P.emit("y", lambda e: e.dma_start(out=onesrow.ap(), in_=hb["onesrow"]), writes=[("onesrow",)], dma=True)
    P.emit("y", lambda e: e.dma_start(out=ropeC.ap(), in_=hb["ropeC"]), writes=[("ropeC",)], dma=True)
    P.emit("y", lambda e: e.dma_start(out=ropeS2.ap(), in_=hb["ropeS2"]), writes=[("ropeS2",)], dma=True)
    P.emit("p", lambda e: e.dma_start(out=ident.ap(), in_=hb["ident"]), writes=[("ident",)], dma=True)
    P.emit("p", lambda e: e.dma_start(out=onesbf.ap(), in_=hb["onesrow"].transpose([1, 0])),
           writes=[("onesbf",)], dma=True)

    def wslice_dma(hbt, nk, cw, m, dst_st, slot, key, extra_reads=()):
        src = hbt[m]
        P.emit("p", lambda e, s=src, d=dst_st[slot].ap():
               e.dma_start(out=d, in_=s),
               reads=[FM] + list(extra_reads), writes=[(key, slot)], dma=True)

    x_sb = SB.alloc("x_sb", [128, 16 * NT], F32)
    for g in range(4):
        srcg = hb["xT"].rearrange("(k p) n -> p k n", p=128)[:, g * 4:(g + 1) * 4, :]
        dstg = x_sb.ap().rearrange("p (k n) -> p k n", n=NT)[:, g * 4:(g + 1) * 4, :]
        P.emit("y", lambda e, s=srcg, d=dstg: e.dma_start(out=d, in_=s),
               writes=[("xs", g)], dma=True)
    xb16 = SB.alloc("xb16", [128, 16 * NT], BF16)
    xq_st = [SB.alloc(f"xq_st{i}", [128, NT], BF16) for i in range(2)]

    def ln_stats_chunk(k, first, last, sum_b0, sq_b0, par=0):
        P.emit("v", lambda e, k=k: e.tensor_copy(
            xb16.ap()[:, k * NT:(k + 1) * NT], x_sb.ap()[:, k * NT:(k + 1) * NT]),
            reads=[("xs", k // 4)], writes=[("xbst", k)])
        P.emit("s", lambda e, k=k, par=par: e.activation(
            xq_st[par].ap(), x_sb.ap()[:, k * NT:(k + 1) * NT], AF.Square),
            reads=[("xs", k // 4)], writes=[("xqst", par)])
        for h in range(2):
            P.emit("t", lambda e, h=h, k=k, f=first, l=last, b=sum_b0: e.matmul(
                psa[0:1, b * 512 + h * 512:b * 512 + (h + 1) * 512], onesbf.ap(),
                xb16.ap()[:, k * NT + h * 512:k * NT + (h + 1) * 512], start=f, stop=l),
                reads=[("xbst", k), ("onesbf",)], writes=bkeys(sum_b0 + h))
            P.emit("t", lambda e, h=h, f=first, l=last, b=sq_b0, par=par: e.matmul(
                psa[0:1, b * 512 + h * 512:b * 512 + (h + 1) * 512], onesbf.ap(),
                xq_st[par].ap()[:, h * 512:(h + 1) * 512], start=f, stop=l),
                reads=[("xqst", par), ("onesbf",)], writes=bkeys(sq_b0 + h))

    def ln_finish(tag, m_b, r_b, sum_b0, sq_b0, bc_b0):
        m_sb = SB.alloc("m_sb" + tag, [1, NT], F32)
        v_sb = SB.alloc("v_sb" + tag, [1, NT], F32)
        msq = SB.alloc("msq" + tag, [1, NT], F32)
        P.emit("s", lambda e: e.activation(m_sb.ap(), psa[0:1, sum_b0 * 512:(sum_b0 + 2) * 512],
                                           AF.Identity, scale=1.0 / D),
               reads=bkeys(sum_b0, 2), writes=[("msb", tag)])
        P.emit("s", lambda e: e.activation(v_sb.ap(), psa[0:1, sq_b0 * 512:(sq_b0 + 2) * 512],
                                           AF.Identity, scale=1.0 / D),
               reads=bkeys(sq_b0, 2), writes=[("vsb", tag)])
        P.emit("s", lambda e: e.activation(msq.ap(), m_sb.ap(), AF.Square),
               reads=[("msb", tag)], writes=[("msq", tag)])
        P.emit("v", lambda e: e.tensor_sub(v_sb.ap(), v_sb.ap(), msq.ap()),
               reads=[("vsb", tag), ("msq", tag)], writes=[("vsb", tag)])
        P.emit("s", lambda e: e.activation(v_sb.ap(), v_sb.ap(), AF.Ln,
                                           bias=consts.ap()[0:1, C_EPS:C_EPS + 1]),
               reads=[("vsb", tag), ("consts",)], writes=[("vsb", tag)])
        P.emit("s", lambda e: e.activation(v_sb.ap(), v_sb.ap(), AF.Exp, scale=-0.5),
               reads=[("vsb", tag)], writes=[("vsb", tag)])
        for h in range(2):
            P.emit("t", lambda e, h=h: e.matmul(
                psa[:, (bc_b0 + h) * 512:(bc_b0 + h + 1) * 512], onesrow.ap(),
                m_sb.ap()[:, h * 512:(h + 1) * 512], start=True, stop=True),
                reads=[("msb", tag), ("onesrow",)], writes=bkeys(bc_b0 + h))
            P.emit("t", lambda e, h=h: e.matmul(
                psa[:, (bc_b0 + 2 + h) * 512:(bc_b0 + 3 + h) * 512], onesrow.ap(),
                v_sb.ap()[:, h * 512:(h + 1) * 512], start=True, stop=True),
                reads=[("vsb", tag), ("onesrow",)], writes=bkeys(bc_b0 + 2 + h))
        P.emit("s", lambda e: e.activation(m_b.ap(), psa[:, bc_b0 * 512:(bc_b0 + 2) * 512], AF.Identity),
               reads=bkeys(bc_b0, 2) + [FM], writes=[("mb", tag)])
        P.emit("s", lambda e: e.activation(r_b.ap(), psa[:, (bc_b0 + 2) * 512:(bc_b0 + 4) * 512], AF.Identity),
               reads=bkeys(bc_b0 + 2, 2) + [FM], writes=[("rb", tag)])
        SB.free_("m_sb" + tag); SB.free_("v_sb" + tag); SB.free_("msq" + tag)
        P.merge_into(FM, {"msb", "vsb", "msq"})

    for k in range(16):
        ln_stats_chunk(k, k == 0, k == 15, 0, 2, par=k % 2)
    m_b1 = SB.alloc("m_b1", [128, NT], BF16)
    r_b1 = SB.alloc("r_b1", [128, NT], BF16)
    ln_finish("ln1", m_b1, r_b1, 0, 2, 4)

    ln1 = SB.alloc("ln1", [128, 16 * NT], BF16)
    t_st = SB.alloc("t_st", [128, NT], BF16)
    ln1_ap = ln1.ap().rearrange("p (k n) -> p k n", n=NT)
    t_st2 = SB.alloc("t_st2", [128, NT], BF16)
    for k in range(16):
        eng = "v" if k % 2 == 0 else "p"
        ts_ = t_st if k % 2 == 0 else t_st2
        tk = ("tst",) if k % 2 == 0 else ("tst2",)
        P.emit(eng, lambda e, k=k, ts_=ts_: e.tensor_sub(
            ts_.ap(), xb16.ap()[:, k * NT:(k + 1) * NT], m_b1.ap()),
            reads=[("xbst", k), ("mb", "ln1"), FM], writes=[tk])
        P.emit(eng, lambda e, k=k, ts_=ts_: e.tensor_mul(
            ln1.ap()[:, k * NT:(k + 1) * NT], ts_.ap(), r_b1.ap()),
            reads=[tk, ("rb", "ln1")], writes=[("ln1", k)])
    SB.free_("t_st2")
    SB.free_("xb16"); SB.free_("m_b1"); SB.free_("r_b1"); SB.free_("x_sb")
    SB.free_("xq_st0"); SB.free_("xq_st1"); SB.free_("t_st")
    P.merge_into(FM, {"xs", "mb", "rb", "xbst", "xqst", "tst", "tst2", "msb", "vsb", "msq"})

    def pi_dst(tensor_ap, h, nblk=NBLK):
        return tensor_ap.rearrange("p (b w) -> p b w", w=128)[:, 0:nblk, h * 8:h * 8 + 8]

    def emit_chain(out2banks, lhs_fn, rhs_fn, nk, reads, writes, chunk_keys=None):
        if chunk_keys is None:
            fns = []
            for h in range(2):
                for k in range(nk):
                    fns.append(lambda e, k=k, h=h, st=(k == 0), sp=(k == nk - 1): e.matmul(
                        out2banks[:, h * 512:(h + 1) * 512], lhs_fn(k),
                        rhs_fn(k)[:, h * 512:(h + 1) * 512], start=st, stop=sp))
            P.emit("t", fns, reads=reads, writes=writes)
            return
        for h in range(2):
            for k in range(nk):
                P.emit("t", lambda e, k=k, h=h, st=(k == 0), sp=(k == nk - 1): e.matmul(
                    out2banks[:, h * 512:(h + 1) * 512], lhs_fn(k),
                    rhs_fn(k)[:, h * 512:(h + 1) * 512], start=st, stop=sp),
                    reads=list(reads) + [chunk_keys(k)], writes=writes)

    cq_sb = SB.alloc("cq_sb", [128, 4 * NT], BF16)
    ckv_sb = SB.alloc("ckv_sb", [128, 4 * NT], BF16)
    krr = SB.alloc("krr", [73, 128 * NBLK], BF16)
    srcv = maskuv.ap()[:, None, 128:256].broadcast_to([9, NBLK, 128])
    P.emit("s", lambda e, d=krr.ap()[64:73, :].rearrange("p (b z) -> p b z", z=128), s=srcv:
           e.activation(d, s, AF.Identity),
           reads=[("maskuv",), FM], writes=[("krrm",)])
    rp_a = [SB.alloc(f"rp_a{i}", [128, NT], F32) for i in range(2)]
    rp_b = [SB.alloc(f"rp_b{i}", [128, NT], F32) for i in range(2)]
    wb_st = [SB.alloc(f"wb_st{i}", [128, 16 * 128], BF16) for i in range(3)]

    ln1_reads = [("ln1", k) for k in range(16)]

    def rope_emit(j, pre_b0, sw_b0, bias_pre, bias_sw, dst_tensor, dst_tag):
        par = j % 2
        P.emit("v", lambda e, bp=bias_pre, b0=pre_b0, par=par: e.scalar_tensor_tensor(
            rp_a[par].ap(), bank(b0, 2), ccol(bp), ropeC.ap(), op0=ALU.add, op1=ALU.mult),
            reads=bkeys(pre_b0, 2) + [("consts",), ("ropeC",), FM], writes=[("rpa", par)])
        P.emit("v", lambda e, bs=bias_sw, b0=sw_b0, par=par: e.scalar_tensor_tensor(
            rp_b[par].ap(), bank(b0, 2), ccol(bs), ropeS2.ap(), op0=ALU.add, op1=ALU.mult),
            reads=bkeys(sw_b0, 2) + [("consts",), ("ropeS2",), FM], writes=[("rpb", par)])
        for hh in range(2):
            head = 2 * j + hh
            dst = pi_dst(dst_tensor.ap()[0:64, :], head)
            src_a = rp_a[par].ap()[hh * 64:(hh + 1) * 64, :].rearrange("p (b z) -> p b z", z=8)
            src_b = rp_b[par].ap()[hh * 64:(hh + 1) * 64, :].rearrange("p (b z) -> p b z", z=8)
            P.emit("v", lambda e, d=dst, a=src_a, b=src_b: e.tensor_add(d, a, b),
                   reads=[("rpa", par), ("rpb", par), FM], writes=[(dst_tag, head)])

    b_order = list(range(8)) + [m for j in range(8) for m in (8 + j, 16 + j)]
    for idx, m in enumerate(b_order):
        sl = idx % 3
        wslice_dma(hb["WB"], 16, 128, m, wb_st, sl, "wbc")
        if m < 8:
            b0 = (idx % 2) * 2
        elif m < 16:
            b0 = 4
        else:
            b0 = 6
        emit_chain(bank(b0, 2), lambda k, sl=sl: wb_st[sl].ap()[:, k * 128:(k + 1) * 128],
                   lambda k: ln1_ap[:, k, :], 16,
                   reads=[("wbc", sl)] + ([] if idx < 2 else ln1_reads), writes=bkeys(b0, 2),
                   chunk_keys=(lambda k: ("ln1", k)) if idx < 2 else None)
        if m < 4:
            P.emit("s", lambda e, m=m, b0=b0: e.activation(
                cq_sb.ap()[:, m * NT:(m + 1) * NT], bank(b0, 2), AF.Identity, bias=ccol(C_BC + m)),
                reads=bkeys(b0, 2) + [("consts",), FM], writes=[("cq", m)])
        elif m < 8:
            P.emit("s", lambda e, m=m, b0=b0: e.activation(
                ckv_sb.ap()[:, (m - 4) * NT:(m - 3) * NT], bank(b0, 2), AF.Identity,
                bias=ccol(C_BCKV + m - 4)),
                reads=bkeys(b0, 2) + [("consts",), FM], writes=[("ckv", m - 4)])
        elif m >= 16:
            j = m - 16
            rope_emit(j, 4, 6, C_BKR + j, C_BKRSW + j, krr, "krr")

    tap("cq", cq_sb.ap(), [128, 4 * NT], BF16, [("cq", c) for c in range(4)])
    tap("ckv", ckv_sb.ap(), [128, 4 * NT], BF16, [("ckv", c) for c in range(4)])
    tap("krr", krr.ap(), [64, 128 * NBLK], BF16, [("krr", h) for h in range(16)])
    SB.free_("ln1")
    for i in range(3):
        SB.free_(f"wb_st{i}")
    P.merge_into(FM, {"ln1", "wbc"})

    kn = SB.alloc("kn", [128, 128 * NBLK], BF16)
    vnat = SB.alloc("vnat", [128, 128 * NBLK], BF16)
    vt_st = SB.alloc("vt_st", [128, 128 * NBLK], BF16)
    wd_st = [SB.alloc(f"wd_st{i}", [128, 4 * 512], BF16) for i in range(2)]

    def ckv_rhs(k):
        return ckv_sb.ap()[:, k * NT:(k + 1) * NT]

    ckv_reads = [("ckv", c) for c in range(4)]
    for h in range(16):
        sl = (h // 4) % 2
        if h % 4 == 0:
            wslice_dma(hb["WD"], 4, 512, h // 4, wd_st, sl, "wdc")
        sub = h % 4
        b0 = (h % 3) * 2
        emit_chain(bank(b0, 2),
                   lambda k, sl=sl, sub=sub: wd_st[sl].ap()[:, k * 512 + sub * 128:k * 512 + (sub + 1) * 128],
                   ckv_rhs, 4, reads=[("wdc", sl)] + ckv_reads, writes=bkeys(b0, 2))
        dst = pi_dst(kn.ap(), h)
        P.emit("s", lambda e, h=h, b0=b0, d=dst: e.activation(
            d, bank(b0, 2).rearrange("p (b z) -> p b z", z=8), AF.Identity, bias=ccol(C_BCK + h)),
            reads=bkeys(b0, 2) + [("consts",), FM], writes=[("kn", h)])
    for h in range(16):
        sl = (h // 4) % 2
        if h % 4 == 0:
            wslice_dma(hb["WD"], 4, 512, 4 + h // 4, wd_st, sl, "wdc")
        sub = h % 4
        b0 = 4 + (h % 2) * 2
        emit_chain(bank(b0, 2),
                   lambda k, sl=sl, sub=sub: wd_st[sl].ap()[:, k * 512 + sub * 128:k * 512 + (sub + 1) * 128],
                   ckv_rhs, 4, reads=[("wdc", sl)] + ckv_reads, writes=bkeys(b0, 2))
        dst = pi_dst(vt_st.ap(), h)
        P.emit("s", lambda e, h=h, b0=b0, d=dst: e.activation(
            d, bank(b0, 2).rearrange("p (b z) -> p b z", z=8), AF.Identity, bias=ccol(C_BV + h)),
            reads=bkeys(b0, 2) + [("consts",), FM], writes=[("vts", h)])
    vts_reads = [("vts", h) for h in range(16)]
    for grp in range(NBLK // 8):
        pb = grp % 2
        fns = []
        for j in range(8):
            bb = grp * 8 + j
            fns.append(lambda e, bb=bb, j=j, pb=pb: e.transpose(
                bank(pb, 1, BF16)[:, j * 128:(j + 1) * 128],
                vt_st.ap()[:, bb * 128:(bb + 1) * 128], ident.ap()))
        P.emit("t", fns, reads=vts_reads + [("ident",)], writes=bkeys(pb))
        P.emit("s", lambda e, grp=grp, pb=pb: e.activation(
            vnat.ap()[:, grp * 1024:(grp + 1) * 1024], bank(pb, 1, BF16), AF.Identity),
            reads=bkeys(pb) + [FM], writes=[("vnat8", grp)])
    tap("kn", kn.ap(), [128, 128 * NBLK], BF16, [("kn", h) for h in range(16)])
    tap("vnat", vnat.ap(), [128, 128 * NBLK], BF16, [("vnat", g) for g in range(NBLK)])
    SB.free_("vt_st"); SB.free_("ckv_sb")
    P.merge_into(FM, {"vts", "ckv", "wdc"})
    for i in range(2):
        SB.free_(f"wd_st{i}")

    qn = SB.alloc("qn", [128, 128 * NBLK], BF16)
    qrr = SB.alloc("qrr", [73, 128 * NBLK], BF16)
    srcu = maskuv.ap()[:, None, 0:128].broadcast_to([9, NBLK, 128])
    P.emit("s", lambda e, d=qrr.ap()[64:73, :].rearrange("p (b z) -> p b z", z=128), s=srcu:
           e.activation(d, s, AF.Identity),
           reads=[("maskuv",), FM], writes=[("qrrm",)])
    wc_st = [SB.alloc(f"wc_st{i}", [128, 4 * 512], BF16) for i in range(2)]

    def cq_rhs(k):
        return cq_sb.ap()[:, k * NT:(k + 1) * NT]

    cq_reads = [("cq", c) for c in range(4)]
    c_order = list(range(16)) + [m for j in range(8) for m in (16 + j, 24 + j)]
    wc_slot = {0: 0, 1: 1, 2: 0, 3: 1, 4: 0, 5: 0, 6: 1, 7: 1}
    wc_loaded = set()
    for idx, m in enumerate(c_order):
        slj = m // 4
        sl = wc_slot[slj]
        if slj not in wc_loaded:
            wc_loaded.add(slj)
            wslice_dma(hb["WC"], 4, 512, slj, wc_st, sl, "wcc")
        sub = m % 4
        if m < 16:
            b0 = (idx % 3) * 2
        elif m < 24:
            b0 = 4
        else:
            b0 = 6
        emit_chain(bank(b0, 2),
                   lambda k, sl=sl, sub=sub: wc_st[sl].ap()[:, k * 512 + sub * 128:k * 512 + (sub + 1) * 128],
                   cq_rhs, 4, reads=[("wcc", sl)] + cq_reads, writes=bkeys(b0, 2))
        if m < 16:
            dst = pi_dst(qn.ap(), m)
            P.emit("s", lambda e, m=m, b0=b0, d=dst: e.activation(
                d, bank(b0, 2).rearrange("p (b z) -> p b z", z=8), AF.Identity, bias=ccol(C_BCQ + m)),
                reads=bkeys(b0, 2) + [("consts",), FM], writes=[("qn", m)])
        elif m >= 24:
            j = m - 24
            rope_emit(j, 4, 6, C_BQR + j, C_BQRSW + j, qrr, "qrr")

    tap("qn", qn.ap(), [128, 128 * NBLK], BF16, [("qn", h) for h in range(16)])
    tap("qrr", qrr.ap(), [64, 128 * NBLK], BF16, [("qrr", h) for h in range(16)])
    SB.free_("cq_sb"); SB.free_("rp_a0"); SB.free_("rp_a1"); SB.free_("rp_b0"); SB.free_("rp_b1")
    SB.free_("ropeC"); SB.free_("ropeS2")
    for i in range(2):
        SB.free_(f"wc_st{i}")
    P.merge_into(FM, {"cq", "rpa", "rpb", "ropeC", "ropeS2", "wcc"})

    woin = SB.alloc("woin", [128, 16 * NT], BF16)
    smx = SB.alloc("smx", [128, 6144], BF16)
    _sap = smx.ap()

    def carve(off_b, cols, dtype):
        n = cols * mybir.dt.size(dtype) // 2
        return _sap[:, off_b // 2: off_b // 2 + n].bitcast(dtype)

    e_bf = [carve(i * 2048, 1024, BF16) for i in range(2)]
    attn_st = [carve(4096 + i * 2048, 1024, BF16) for i in range(2)]
    attnT_st = [carve(8192 + i * 2048, 1024, BF16) for i in range(2)]
    mx_sb = SB.alloc("mx_sb", [128, 32], F32)
    mx_st = [mx_sb.ap()[:, i * 16:(i + 1) * 16] for i in range(2)]

    qn_reads = [("qn", h) for h in range(16)]
    kn_reads = [("kn", h) for h in range(16)]
    qr_reads = [("qrr", h) for h in range(16)] + [("krr", h) for h in range(16)]

    def emit_scores(g):
        sb0 = (g % 2) * 2
        fns = []
        for j in range(8):
            blk = g * 8 + j
            col = slice(blk * 128, (blk + 1) * 128)
            fns.append(lambda e, col=col, j=j, sb0=sb0: e.matmul(
                bank(sb0, 2)[:, j * 128:(j + 1) * 128], qn.ap()[:, col], kn.ap()[:, col],
                start=True, stop=False))
            fns.append(lambda e, col=col, j=j, sb0=sb0: e.matmul(
                bank(sb0, 2)[:, j * 128:(j + 1) * 128], qrr.ap()[0:73, col], krr.ap()[0:73, col],
                start=False, stop=True))
        P.emit("t", fns, reads=qn_reads + kn_reads + qr_reads + [("qrrm",), ("krrm",), FM],
               writes=bkeys(sb0, 2))

    def emit_subexp(g):
        p = g % 2
        sb0 = p * 2
        sc3 = bank(sb0, 2).rearrange("p (b z) -> p b z", z=128)
        P.emit("v", lambda e, sc3=sc3, p=p: e.reduce_max(
            mx_st[p][:, 0:8].rearrange("p (b z) -> p b z", z=1), sc3, axis=AX.X),
            reads=bkeys(sb0, 2), writes=[("mx", p)])
        mxb = mx_st[p][:, 0:8][:, :, None].broadcast_to([128, 8, 128])
        P.emit("v", lambda e, sc3=sc3, mxb=mxb, p=p: e.tensor_sub(
            e_bf[p].rearrange("p (b z) -> p b z", z=128), sc3, mxb),
            reads=bkeys(sb0, 2) + [("mx", p), FM], writes=[("ebf", p)])
        P.emit("s", lambda e, p=p: e.activation(e_bf[p], e_bf[p], AF.Exp, scale=SCALER),
               reads=[("ebf", p), FM], writes=[("ebf", p)])

    def emit_soft(g):
        p = g % 2
        P.emit("v", lambda e, p=p: e.reduce_sum(
            mx_st[p][:, 8:16].rearrange("p (b z) -> p b z", z=1),
            e_bf[p].rearrange("p (b z) -> p b z", z=128), axis=AX.X),
            reads=[("ebf", p)], writes=[("sums", p)])
        P.emit("v", lambda e, p=p: e.reciprocal(mx_st[p][:, 8:16], mx_st[p][:, 8:16]),
               reads=[("sums", p)], writes=[("sums", p)])
        rsb = mx_st[p][:, 8:16][:, :, None].broadcast_to([128, 8, 128])
        P.emit("p", lambda e, rsb=rsb, p=p: e.tensor_mul(
            attn_st[p].rearrange("p (b z) -> p b z", z=128),
            e_bf[p].rearrange("p (b z) -> p b z", z=128), rsb),
            reads=[("ebf", p), ("sums", p), FM], writes=[("attn", p)])

    def emit_av(g):
        p = g % 2
        tb = 4 if p == 0 else 7
        fns = []
        for j in range(8):
            fns.append(lambda e, j=j, p=p, tb=tb: e.transpose(
                bank(tb, 1, BF16)[:, j * 128:(j + 1) * 128],
                attn_st[p][:, j * 128:(j + 1) * 128], ident.ap()))
        P.emit("t", fns, reads=[("attn", p), ("ident",)], writes=bkeys(tb))
        P.emit("s", lambda e, p=p, tb=tb: e.activation(attnT_st[p], bank(tb, 1, BF16), AF.Identity),
               reads=bkeys(tb) + [FM], writes=[("attnT", p)])
        fns = []
        for j in range(8):
            blk = g * 8 + j
            fns.append(lambda e, j=j, blk=blk, p=p: e.matmul(
                bank(5, 2)[:, j * 128:(j + 1) * 128], vnat.ap()[:, blk * 128:(blk + 1) * 128],
                attnT_st[p][:, j * 128:(j + 1) * 128], start=True, stop=True))
        P.emit("t", fns, reads=[("attnT", p), ("vnat8", g)], writes=bkeys(5, 2))
        dstw = woin.ap().rearrange("p (i t) -> p i t", t=NT)[:, :, g * 64:(g + 1) * 64] \
                   .rearrange("p i (j z) -> p j i z", z=8)
        srcp = bank(5, 2).rearrange("p (j i z) -> p j i z", i=16, z=8)
        P.emit("s", lambda e, d=dstw, s=srcp: e.activation(d, s, AF.Identity),
            reads=bkeys(5, 2) + [FM], writes=[("woing", g)])

    emit_scores(0)
    emit_subexp(0)
    if True:
        emit_scores(1)
    for g in range(16):
        if g + 1 < 16:
            emit_subexp(g + 1)
        if g + 2 < 16:
            emit_scores(g + 2)
        emit_soft(g)
        emit_av(g)

    for nm in ("qn", "kn", "qrr", "krr", "vnat"):
        SB.free_(nm)
    SB.free_("smx"); SB.free_("mx_sb")
    P.merge_into(FM, {"qn", "kn", "qrr", "krr", "vnat8", "scst", "ebf", "mest", "attn", "attnT", "mx", "sums"})

    tap("woin", woin.ap(), [128, 16 * NT], BF16, [("woin", h) for h in range(16)])
    wo_st = [SB.alloc(f"wo_st{i}", [128, 16 * 128], BF16) for i in range(3)]
    xch_st = [SB.alloc(f"xch_st{i}", [128, NT], F32) for i in range(2)]
    x2_st = [SB.alloc(f"x2_st{i}", [128, NT], F32) for i in range(2)]
    x2b16 = SB.alloc("x2b16", [128, 16 * NT], BF16)
    xq2_st = [SB.alloc(f"xq2_st{i}", [128, NT], BF16) for i in range(2)]
    woin_reads = [("woing", g) for g in range(16)]

    x2s_r = hb["x2s"].rearrange("(m p) n -> p m n", p=128)
    xT_r = hb["xT"].rearrange("(m p) n -> p m n", p=128)
    for m in range(16):
        sl = m % 2
        wsl = m % 3
        wslice_dma(hb["WO"], 16, 128, m, wo_st, wsl, "wos")
        P.emit("y", lambda e, m=m, sl=sl: e.dma_start(out=xch_st[sl].ap(), in_=xT_r[:, m, :]),
               reads=[FM], writes=[("xch", sl)], dma=True)
        b0 = (m % 2) * 2
        fns = []
        for hh in range(2):
            for k in range(16):
                fns.append(lambda e, k=k, hh=hh, wsl=wsl, b0=b0, st=(k == 0): e.matmul(
                    bank(b0, 2)[:, hh * 512:(hh + 1) * 512],
                    wo_st[wsl].ap()[:, k * 128:(k + 1) * 128],
                    woin.ap()[:, k * NT + hh * 512: k * NT + (hh + 1) * 512],
                    start=st, stop=(k == 15)))
        P.emit("t", fns, reads=[("wos", wsl)] + woin_reads, writes=bkeys(b0, 2))
        P.emit("v", lambda e, m=m, sl=sl, b0=b0: e.scalar_tensor_tensor(
            x2_st[sl].ap(), bank(b0, 2), ccol(C_BO + m), xch_st[sl].ap(),
            op0=ALU.add, op1=ALU.add),
            reads=bkeys(b0, 2) + [("xch", sl), ("consts",), FM], writes=[("x2st", sl)])
        P.emit("y", lambda e, m=m, sl=sl: e.dma_start(out=x2s_r[:, m, :], in_=x2_st[sl].ap()),
               reads=[("x2st", sl)], writes=[("x2h", m)], dma=True)
        P.emit("s", lambda e, m=m, sl=sl: e.activation(
            x2b16.ap()[:, m * NT:(m + 1) * NT], x2_st[sl].ap(), AF.Identity),
            reads=[("x2st", sl), FM], writes=[("xbst", m)])

    for k in range(16):
        par = k % 2
        P.emit("s", lambda e, k=k, par=par: e.activation(
            xq2_st[par].ap(), x2b16.ap()[:, k * NT:(k + 1) * NT], AF.Square),
            reads=[("xbst", k)], writes=[("xqst", par)])
        for h in range(2):
            P.emit("t", lambda e, h=h, k=k, st=(k == 0): e.matmul(
                psa[0:1, 2048 + h * 512:2048 + (h + 1) * 512], onesbf.ap(),
                x2b16.ap()[:, k * NT + h * 512:k * NT + (h + 1) * 512], start=st, stop=(k == 15)),
                reads=[("xbst", k), ("onesbf",)], writes=bkeys(4 + h))
            P.emit("t", lambda e, h=h, par=par, st=(k == 0): e.matmul(
                psa[0:1, 3072 + h * 512:3072 + (h + 1) * 512], onesbf.ap(),
                xq2_st[par].ap()[:, h * 512:(h + 1) * 512], start=st, stop=(k == 15)),
                reads=[("xqst", par), ("onesbf",)], writes=bkeys(6 + h))
    for nm in ("woin", "wo_st0", "wo_st1", "wo_st2", "xq2_st0", "xq2_st1", "x2_st0", "x2_st1"):
        SB.free_(nm)
    P.merge_into(FM, {"woin", "woing", "wos", "xqst", "x2st"})

    if "x2" in taps:
        dbg_x2 = nc.dram_tensor("dbg_x2", [D, NT], F32, kind="ExternalOutput").ap()
        P.emit("y", lambda e: e.dma_start(out=dbg_x2, in_=hb["x2s"]),
               reads=[("x2h", m) for m in range(16)], writes=[("tapk", "x2")], dma=True)
        tap_keys.append(("tapk", "x2"))

    m_b2 = SB.alloc("m_b2", [128, NT], BF16)
    r_b2 = SB.alloc("r_b2", [128, NT], BF16)
    ln_finish("ln2", m_b2, r_b2, 4, 6, 0)

    ln2 = SB.alloc("ln2", [128, 16 * NT], BF16)
    t2_st = SB.alloc("t2_st", [128, NT], BF16)
    ln2_ap = ln2.ap().rearrange("p (k n) -> p k n", n=NT)
    t2b_st = SB.alloc("t2b_st", [128, NT], BF16)
    for k in range(16):
        eng = "v" if k % 2 == 0 else "p"
        ts_ = t2_st if k % 2 == 0 else t2b_st
        tk = ("tst",) if k % 2 == 0 else ("tst2",)
        P.emit(eng, lambda e, k=k, ts_=ts_: e.tensor_sub(
            ts_.ap(), x2b16.ap()[:, k * NT:(k + 1) * NT], m_b2.ap()),
            reads=[("xbst", k), ("mb", "ln2"), FM], writes=[tk])
        P.emit(eng, lambda e, k=k, ts_=ts_: e.tensor_mul(
            ln2.ap()[:, k * NT:(k + 1) * NT], ts_.ap(), r_b2.ap()),
            reads=[tk, ("rb", "ln2")], writes=[("ln2", k)])
    SB.free_("t2b_st"); SB.free_("x2b16")
    tap("ln2", ln2.ap(), [128, 16 * NT], BF16, [("ln2", k) for k in range(16)])
    tap("mb2", m_b2.ap(), [128, NT], F32, [("mb", "ln2")])
    SB.free_("m_b2"); SB.free_("r_b2"); SB.free_("t2_st")
    P.merge_into(FM, {"mb", "rb", "tst", "tst2", "xbst", "msb", "vsb", "msq"})

    gated4 = [SB.alloc(f"gated{i}", [128, 16 * NT], BF16) for i in range(4)]
    wgu_st = [SB.alloc(f"wgu_st{i}", [128, 2 * 16 * 128], BF16) for i in range(2)]

    def gated_ap(c):
        return gated4[c // 16].ap()[:, (c % 16) * NT:(c % 16 + 1) * NT]
    sg_st = SB.alloc("sg_st", [128, NT], F32)
    ln2_reads = [("ln2", k) for k in range(16)]
    wdn_st = [SB.alloc("wdn_st0", [128, 64 * 128], BF16), None]
    wslice_dma(hb["WDN"], 64, 128, 0, wdn_st, 0, "wdn")
    for c in range(64):
        sl = c % 2
        wslice_dma(hb["WGU"], 32, 128, c, wgu_st, sl, "wgu")
        bg = (c % 2) * 4
        bu = bg + 2
        for half_w, b0 in ((0, bg), (1, bu)):
            fns = []
            for hh in range(2):
                for k in range(16):
                    fns.append(lambda e, k=k, hh=hh, sl=sl, b0=b0, hw=half_w, st=(k == 0): e.matmul(
                        bank(b0, 2)[:, hh * 512:(hh + 1) * 512],
                        wgu_st[sl].ap()[:, hw * 2048 + k * 128:hw * 2048 + (k + 1) * 128],
                        ln2_ap[:, k, hh * 512:(hh + 1) * 512], start=st, stop=(k == 15)))
            if c < 2:
                for hh in range(2):
                    for k in range(16):
                        P.emit("t", lambda e, k=k, hh=hh, sl=sl, b0=b0, hw=half_w, st=(k == 0): e.matmul(
                            bank(b0, 2)[:, hh * 512:(hh + 1) * 512],
                            wgu_st[sl].ap()[:, hw * 2048 + k * 128:hw * 2048 + (k + 1) * 128],
                            ln2_ap[:, k, hh * 512:(hh + 1) * 512], start=st, stop=(k == 15)),
                            reads=[("wgu", sl), ("ln2", k)], writes=bkeys(b0, 2))
            else:
                P.emit("t", fns, reads=[("wgu", sl)] + ln2_reads, writes=bkeys(b0, 2))
        P.emit("s", lambda e, bg=bg: e.activation(sg_st.ap(), bank(bg, 2), AF.Silu),
               reads=bkeys(bg, 2), writes=[("sg",)])
        P.emit("v", lambda e, c=c, bu=bu: e.tensor_mul(
            gated_ap(c), sg_st.ap(), bank(bu, 2)),
            reads=[("sg",), FM] + bkeys(bu, 2), writes=[("gated", c)])

    for gi in range(4):
        tap(f"gated{gi}", gated4[gi].ap(), [128, 16 * NT], BF16,
            [("gated", c) for c in range(gi * 16, gi * 16 + 16)])
    SB.free_("ln2")
    for nm in ("wgu_st0", "wgu_st1", "sg_st"):
        SB.free_(nm)
    P.merge_into(FM, {"ln2", "wg", "wu", "sg"})

    wdn_st[1] = SB.alloc("wdn_st1", [128, 64 * 128], BF16)
    out_st = [SB.alloc(f"out_st{i}", [128, NT], F32) for i in range(2)]
    outT_r = hb["outT"].rearrange("(m p) n -> p m n", p=128)
    gated_reads = [("gated", c) for c in range(64)]
    for m in range(16):
        sl = m % 2
        wsl = m % 2
        if m > 0:
            wslice_dma(hb["WDN"], 64, 128, m, wdn_st, wsl, "wdn")
        P.emit("y", lambda e, m=m, sl=sl: e.dma_start(out=xch_st[sl].ap(), in_=x2s_r[:, m, :]),
               reads=[("x2h", m), FM], writes=[("xch", sl)], dma=True)
        b0 = (m % 2) * 2
        fns = []
        for hh in range(2):
            for k in range(64):
                fns.append(lambda e, k=k, hh=hh, wsl=wsl, b0=b0, st=(k == 0): e.matmul(
                    bank(b0, 2)[:, hh * 512:(hh + 1) * 512],
                    wdn_st[wsl].ap()[:, k * 128:(k + 1) * 128],
                    gated_ap(k)[:, hh * 512:(hh + 1) * 512],
                    start=st, stop=(k == 63)))
        P.emit("t", fns, reads=[("wdn", wsl)] + gated_reads, writes=bkeys(b0, 2))
        P.emit("v", lambda e, sl=sl, b0=b0: e.tensor_add(
            out_st[sl].ap(), bank(b0, 2), xch_st[sl].ap()),
            reads=bkeys(b0, 2) + [("xch", sl), FM], writes=[("outst", sl)])
        P.emit("y", lambda e, m=m, sl=sl: e.dma_start(out=outT_r[:, m, :], in_=out_st[sl].ap()),
               reads=[("outst", sl)], writes=[("outh", m)], dma=True)

    P.emit("y", lambda e: e.nop(), reads=[("outh", m) for m in range(16)] + tap_keys)

    P.replay()
    P.stack.close()
    return nc


def _rope_tables(core):
    pos0 = (core % 2) * NT
    t = np.arange(pos0, pos0 + NT, dtype=np.float64)
    inv = 1.0 / (10000.0 ** (np.arange(0, DRH, 2, dtype=np.float64) / DRH))
    ang = t[None, :] * inv[:, None]
    cos64 = np.concatenate([np.cos(ang), np.cos(ang)], axis=0)
    s2_64 = np.concatenate([-np.sin(ang), np.sin(ang)], axis=0)
    cosT = np.concatenate([cos64, cos64], axis=0).astype(np.float32)
    s2T = np.concatenate([s2_64, s2_64], axis=0).astype(np.float32)
    return np.ascontiguousarray(cosT), np.ascontiguousarray(s2T)


def _swap_cols(w):
    w = w.reshape(w.shape[0], -1, 64)
    return np.ascontiguousarray(
        np.concatenate([w[:, :, 32:], w[:, :, :32]], axis=2).reshape(w.shape[0], -1))


def _swap_bias(b):
    b = b.reshape(-1, 64)
    return np.ascontiguousarray(np.concatenate([b[:, 32:], b[:, :32]], axis=1).reshape(-1))


def _prep_shared(i):
    f32 = lambda a: np.ascontiguousarray(np.asarray(a, dtype=np.float32))
    Wc, Wckv = f32(i["L0_attn_Wc"]), f32(i["L0_attn_Wckv"])
    Wkr = f32(i["L0_attn_Wkr"])
    Wcq, Wqr = f32(i["L0_attn_Wcq"]), f32(i["L0_attn_Wqr"])
    Wck, Wv = f32(i["L0_attn_Wck"]), f32(i["L0_attn_Wv"])
    def repack(W, cw=128):
        K, N = W.shape
        nk, nm = K // 128, N // cw
        return np.ascontiguousarray(
            W.reshape(nk, 128, nm, cw).transpose(2, 1, 0, 3).reshape(nm, 128, nk * cw))

    sh = {}
    sh["WB"] = repack(np.concatenate([Wc, Wckv, Wkr, _swap_cols(Wkr)], axis=1))
    sh["WC"] = repack(np.concatenate([Wcq, Wqr, _swap_cols(Wqr)], axis=1), cw=512)
    sh["WD"] = repack(np.concatenate([Wck, Wv], axis=1), cw=512)
    sh["WO"] = repack(f32(i["L0_attn_Wo"]))
    wg_r = repack(np.ascontiguousarray(f32(i["L0_ffn_w_gate"]).T))
    wu_r = repack(np.ascontiguousarray(f32(i["L0_ffn_w_up"]).T))
    sh["WGU"] = np.ascontiguousarray(np.concatenate([wg_r, wu_r], axis=2))
    sh["WDN"] = repack(np.ascontiguousarray(f32(i["L0_ffn_w_down"]).T))
    consts = np.zeros((128, NCONST), dtype=np.float32)
    consts[:, C_EPS] = EPS
    consts[:, C_ONE] = 1.0

    def put(col0, vec):
        v = np.asarray(vec, dtype=np.float32).reshape(-1, 128)
        for c in range(v.shape[0]):
            consts[:, col0 + c] = v[c]

    put(C_BC, i["L0_attn_bc"]); put(C_BCKV, i["L0_attn_bckv"])
    put(C_BKR, i["L0_attn_bkr"]); put(C_BKRSW, _swap_bias(f32(i["L0_attn_bkr"])))
    put(C_BCQ, i["L0_attn_bcq"])
    put(C_BQR, i["L0_attn_bqr"]); put(C_BQRSW, _swap_bias(f32(i["L0_attn_bqr"])))
    put(C_BCK, i["L0_attn_bck"]); put(C_BV, i["L0_attn_bv"])
    put(C_BO, i["L0_attn_bo"])
    sh["consts"] = consts
    r = np.arange(128)
    uv = np.zeros((9, 256), dtype=np.float32)
    for rr in range(8):
        uv[rr, 0:128] = (r % 8 == rr)
        uv[rr, 128:256] = 416.0 * (r % 8 == rr)
    uv[8, 0:128] = 1.0
    uv[8, 128:256] = -416.0
    sh["maskuv"] = uv
    sh["ident"] = np.eye(128, dtype=np.float32)
    sh["onesrow"] = np.ones((1, 128), dtype=np.float32)
    return sh


def kernel(**inputs):
    x = np.asarray(inputs["L0_attn_h_t"], dtype=np.float32)
    xf = x.reshape(NCORE * NT, D)
    sh = _prep_shared(inputs)
    nc = build_program()
    in_maps = []
    for c in range(NCORE):
        cosT, s2T = _rope_tables(c)
        m = dict(sh)
        m["xT"] = np.ascontiguousarray(xf[c * NT:(c + 1) * NT].T)
        m["ropeC"] = cosT
        m["ropeS2"] = s2T
        in_maps.append(m)
    res = run_bass_kernel_spmd(nc, in_maps, core_ids=list(range(NCORE)))
    out = np.empty((NCORE * NT, D), dtype=np.float32)
    for c in range(NCORE):
        out[c * NT:(c + 1) * NT] = np.asarray(res.results[c]["outT"]).T
    return out.reshape(4, SEQ, D)


# version 28
# speedup vs baseline: 1.0184x; 1.0184x over previous
import math
import numpy as np
from contextlib import ExitStack

import concourse.bass as bass
import concourse.mybir as mybir
from concourse.bass_utils import run_bass_kernel_spmd

F32 = mybir.dt.float32
BF16 = mybir.dt.bfloat16
AF = mybir.ActivationFunctionType
ALU = mybir.AluOpType
AX = mybir.AxisListType

NT = 1024
D = 2048
DC = 512
DCKV = 512
H = 16
DHD = 128
DRH = 64
HID = 8192
NBLK = NT // 8
SCALER = 1.0 / math.sqrt(DHD + DRH)
EPS = 1e-5
SEQ = 2048
NCORE = 8

NCONST = 106
C_EPS, C_ONE = 0, 1
C_BC, C_BCKV, C_BKR, C_BKRSW = 2, 6, 10, 18
C_BCQ, C_BQR, C_BQRSW = 26, 42, 50
C_BCK, C_BV, C_BO = 58, 74, 90


FM = ("freemem",)


class SbufAlloc:
    def __init__(self, nc, lo=18432, hi=229312):
        self.nc = nc
        self.free = [(lo, hi)]
        self.live = {}

    def alloc(self, name, shape, dtype, align=64):
        per = int(np.prod(shape[1:])) * mybir.dt.size(dtype)
        if per >= 16384:
            for i in range(len(self.free) - 1, -1, -1):
                a, b = self.free[i]
                off = (b - per) // align * align
                if off >= a:
                    if off == a:
                        self.free.pop(i)
                    else:
                        self.free[i] = (a, off)
                    if off + per < b:
                        self.free.insert(i + 1, (off + per, b))
                    self.live[name] = (off, per)
                    return self.nc.alloc_sbuf_tensor_at(name, list(shape), dtype, offset=off)
        else:
            for i, (a, b) in enumerate(self.free):
                off = (a + align - 1) // align * align
                if off + per <= b:
                    if off + per == b:
                        if off == a:
                            self.free.pop(i)
                        else:
                            self.free[i] = (a, off)
                    else:
                        self.free[i] = (off + per, b)
                        if off > a:
                            self.free.insert(i, (a, off))
                    self.live[name] = (off, per)
                    return self.nc.alloc_sbuf_tensor_at(name, list(shape), dtype, offset=off)
        raise MemoryError(f"SBUF alloc failed for {name} ({per}B); free={self.free}")

    def free_(self, name):
        off, per = self.live.pop(name)
        self.free.append((off, off + per))
        self.free.sort()
        merged = []
        for a, b in self.free:
            if merged and merged[-1][1] >= a:
                merged[-1][1] = max(merged[-1][1], b)
            else:
                merged.append([a, b])
        self.free = [tuple(x) for x in merged]


class Prog:

    NLANE = 4

    def __init__(self, nc):
        self.nc = nc
        self.ops = {e: [] for e in "tvsyp"}
        self.sem_names = (["t", "v", "s", "y", "p"] +
                          [f"Y{i}" for i in range(self.NLANE)] +
                          [f"P{i}" for i in range(self.NLANE)])
        self.counts = {sn: 0 for sn in self.sem_names}
        self.dma_idx = {"y": 0, "p": 0}
        self.waited = {e: {} for e in "tvsyp"}
        self.res = {}
        self.stack = ExitStack()
        self.sems = {}

    def open_sems(self):
        for sn in self.sem_names:
            self.sems[sn] = self.stack.enter_context(self.nc.semaphore("sem_" + sn))

    def _dep(self, deps, sem, val):
        if val > deps.get(sem, 0):
            deps[sem] = val

    def emit(self, eng, fns, reads=(), writes=(), dma=False):
        if callable(fns):
            fns = [fns]
        if dma:
            idx = self.dma_idx[eng]
            self.dma_idx[eng] = idx + 1
            lane = idx % self.NLANE
            sem = f"{eng.upper()}{lane}"
            amt = 16
            prior = self.counts[sem]
            if prior > 0 and self.waited[eng].get(sem, 0) < prior:
                self.ops[eng].append(("w", sem, prior))
                self.waited[eng][sem] = prior
        else:
            sem = eng
            amt = 1
        deps = {}
        for k in reads:
            r = self.res.get(k)
            if r and r["w"]:
                self._dep(deps, *r["w"])
            if r and k == FM:
                for s_, v_ in r["r"].items():
                    self._dep(deps, s_, v_)
        for k in writes:
            r = self.res.get(k)
            if r:
                if r["w"]:
                    self._dep(deps, *r["w"])
                for s_, v_ in r["r"].items():
                    self._dep(deps, s_, v_)
        for dsem, dval in sorted(deps.items()):
            if dsem == "t" and eng == "t" and not dma:
                continue
            if self.waited[eng].get(dsem, 0) >= dval:
                continue
            self.ops[eng].append(("w", dsem, dval))
            self.waited[eng][dsem] = dval
        for i, fn in enumerate(fns):
            self.ops[eng].append(("o", fn, sem if i == len(fns) - 1 else None, amt))
        self.counts[sem] += amt
        v = self.counts[sem]
        for k in writes:
            self.res[k] = {"w": (sem, v), "r": {}}
        for k in reads:
            if k == FM:
                continue
            r = self.res.setdefault(k, {"w": None, "r": {}})
            if v > r["r"].get(sem, 0):
                r["r"][sem] = v

    def merge_into(self, dst, prefixes):
        agg = self.res.setdefault(dst, {"w": None, "r": {}})
        for k in list(self.res.keys()):
            if k == dst:
                continue
            tag = k[0] if isinstance(k, tuple) else k
            if tag in prefixes:
                r = self.res.pop(k)
                ents = list(r["r"].items())
                if r["w"]:
                    ents.append(r["w"])
                for s_, v_ in ents:
                    if v_ > agg["r"].get(s_, 0):
                        agg["r"][s_] = v_

    def replay(self):
        block = self.stack.enter_context(self.nc.Block())
        prog = self

        def runner(eng_name):
            def run(engobj):
                for item in prog.ops[eng_name]:
                    if item[0] == "w":
                        engobj.wait_ge(prog.sems[item[1]], item[2])
                    else:
                        inst = item[1](engobj)
                        if item[2] is not None:
                            inst.then_inc(prog.sems[item[2]], item[3])
            return run

        block.tensor(runner("t"))
        block.vector(runner("v"))
        block.scalar(runner("s"))
        block.sync(runner("y"))
        block.gpsimd(runner("p"))


def build_program(taps=()):
    nc = bass.Bass("TRN2", target_bir_lowering=False)
    hb = {}
    hb["xT"] = nc.dram_tensor("xT", [D, NT], F32, kind="ExternalInput").ap()
    hb["consts"] = nc.dram_tensor("consts", [128, NCONST], F32, kind="ExternalInput").ap()
    hb["maskuv"] = nc.dram_tensor("maskuv", [9, 256], F32, kind="ExternalInput").ap()
    hb["ident"] = nc.dram_tensor("ident", [128, 128], F32, kind="ExternalInput").ap()
    hb["onesrow"] = nc.dram_tensor("onesrow", [1, 128], F32, kind="ExternalInput").ap()
    hb["ropeC"] = nc.dram_tensor("ropeC", [128, NT], F32, kind="ExternalInput").ap()
    hb["ropeS2"] = nc.dram_tensor("ropeS2", [128, NT], F32, kind="ExternalInput").ap()
    hb["WB"] = nc.dram_tensor("WB", [24, 128, 16 * 128], F32, kind="ExternalInput").ap()
    hb["WC"] = nc.dram_tensor("WC", [8, 128, 4 * 512], F32, kind="ExternalInput").ap()
    hb["WD"] = nc.dram_tensor("WD", [8, 128, 4 * 512], F32, kind="ExternalInput").ap()
    hb["WO"] = nc.dram_tensor("WO", [16, 128, 16 * 128], F32, kind="ExternalInput").ap()
    hb["WGU"] = nc.dram_tensor("WGU", [64, 128, 2 * 16 * 128], F32, kind="ExternalInput").ap()
    hb["WDN"] = nc.dram_tensor("WDN", [16, 128, 64 * 128], F32, kind="ExternalInput").ap()
    hb["x2s"] = nc.dram_tensor("x2s", [D, NT], F32, kind="Internal").ap()
    hb["outT"] = nc.dram_tensor("outT", [D, NT], F32, kind="ExternalOutput").ap()

    P = Prog(nc)
    P.open_sems()
    SB = SbufAlloc(nc)
    PS = P.stack.enter_context(nc.psum_tensor("PS", [128, 4096], F32))
    psa = PS.ap()

    def bank(i, n=1, dtype=F32):
        ap = psa[:, i * 512:(i + n) * 512]
        return ap if dtype == F32 else ap.bitcast(dtype)

    def bkeys(i, n=1):
        return [("PB", i + j) for j in range(n)]

    tap_keys = []

    def tap(name, sbuf_ap, shape, dtype, reads):
        if name not in taps:
            return
        dt_ = nc.dram_tensor("dbg_" + name, list(shape), dtype, kind="ExternalOutput").ap()
        P.emit("y", lambda e, d=dt_, s=sbuf_ap: e.dma_start(out=d, in_=s),
               reads=list(reads), writes=[("tapk", name)], dma=True)
        tap_keys.append(("tapk", name))

    consts = SB.alloc("consts", [128, NCONST], F32)
    maskuv = SB.alloc("maskuv", [9, 256], BF16)
    ident = SB.alloc("ident", [128, 128], BF16)
    onesrow = SB.alloc("onesrow", [1, 128], F32)
    onesbf = SB.alloc("onesbf", [128, 1], BF16)
    ropeC = SB.alloc("ropeC", [128, NT], F32)
    ropeS2 = SB.alloc("ropeS2", [128, NT], F32)

    def ccol(i):
        return consts.ap()[:, i:i + 1]

    P.emit("y", lambda e: e.dma_start(out=consts.ap(), in_=hb["consts"]), writes=[("consts",)], dma=True)
    P.emit("p", lambda e: e.dma_start(out=maskuv.ap(), in_=hb["maskuv"]), writes=[("maskuv",)], dma=True)
    P.emit("y", lambda e: e.dma_start(out=onesrow.ap(), in_=hb["onesrow"]), writes=[("onesrow",)], dma=True)
    P.emit("y", lambda e: e.dma_start(out=ropeC.ap(), in_=hb["ropeC"]), writes=[("ropeC",)], dma=True)
    P.emit("y", lambda e: e.dma_start(out=ropeS2.ap(), in_=hb["ropeS2"]), writes=[("ropeS2",)], dma=True)
    P.emit("p", lambda e: e.dma_start(out=ident.ap(), in_=hb["ident"]), writes=[("ident",)], dma=True)
    P.emit("p", lambda e: e.dma_start(out=onesbf.ap(), in_=hb["onesrow"].transpose([1, 0])),
           writes=[("onesbf",)], dma=True)

    def wslice_dma(hbt, nk, cw, m, dst_st, slot, key, extra_reads=()):
        src = hbt[m]
        P.emit("p", lambda e, s=src, d=dst_st[slot].ap():
               e.dma_start(out=d, in_=s),
               reads=[FM] + list(extra_reads), writes=[(key, slot)], dma=True)

    x_sb = SB.alloc("x_sb", [128, 16 * NT], F32)
    for g in range(4):
        srcg = hb["xT"].rearrange("(k p) n -> p k n", p=128)[:, g * 4:(g + 1) * 4, :]
        dstg = x_sb.ap().rearrange("p (k n) -> p k n", n=NT)[:, g * 4:(g + 1) * 4, :]
        P.emit("y", lambda e, s=srcg, d=dstg: e.dma_start(out=d, in_=s),
               writes=[("xs", g)], dma=True)
    xb16 = SB.alloc("xb16", [128, 16 * NT], BF16)
    xq_st = [SB.alloc(f"xq_st{i}", [128, NT], BF16) for i in range(2)]

    def ln_stats_chunk(k, first, last, sum_b0, sq_b0, par=0):
        P.emit("v", lambda e, k=k: e.tensor_copy(
            xb16.ap()[:, k * NT:(k + 1) * NT], x_sb.ap()[:, k * NT:(k + 1) * NT]),
            reads=[("xs", k // 4)], writes=[("xbst", k)])
        P.emit("s", lambda e, k=k, par=par: e.activation(
            xq_st[par].ap(), x_sb.ap()[:, k * NT:(k + 1) * NT], AF.Square),
            reads=[("xs", k // 4)], writes=[("xqst", par)])
        for h in range(2):
            P.emit("t", lambda e, h=h, k=k, f=first, l=last, b=sum_b0: e.matmul(
                psa[0:1, b * 512 + h * 512:b * 512 + (h + 1) * 512], onesbf.ap(),
                xb16.ap()[:, k * NT + h * 512:k * NT + (h + 1) * 512], start=f, stop=l),
                reads=[("xbst", k), ("onesbf",)], writes=bkeys(sum_b0 + h))
            P.emit("t", lambda e, h=h, f=first, l=last, b=sq_b0, par=par: e.matmul(
                psa[0:1, b * 512 + h * 512:b * 512 + (h + 1) * 512], onesbf.ap(),
                xq_st[par].ap()[:, h * 512:(h + 1) * 512], start=f, stop=l),
                reads=[("xqst", par), ("onesbf",)], writes=bkeys(sq_b0 + h))

    def ln_finish(tag, m_b, r_b, sum_b0, sq_b0, bc_b0):
        m_sb = SB.alloc("m_sb" + tag, [1, NT], F32)
        v_sb = SB.alloc("v_sb" + tag, [1, NT], F32)
        msq = SB.alloc("msq" + tag, [1, NT], F32)
        P.emit("s", lambda e: e.activation(m_sb.ap(), psa[0:1, sum_b0 * 512:(sum_b0 + 2) * 512],
                                           AF.Identity, scale=1.0 / D),
               reads=bkeys(sum_b0, 2), writes=[("msb", tag)])
        P.emit("s", lambda e: e.activation(v_sb.ap(), psa[0:1, sq_b0 * 512:(sq_b0 + 2) * 512],
                                           AF.Identity, scale=1.0 / D),
               reads=bkeys(sq_b0, 2), writes=[("vsb", tag)])
        P.emit("s", lambda e: e.activation(msq.ap(), m_sb.ap(), AF.Square),
               reads=[("msb", tag)], writes=[("msq", tag)])
        P.emit("v", lambda e: e.tensor_sub(v_sb.ap(), v_sb.ap(), msq.ap()),
               reads=[("vsb", tag), ("msq", tag)], writes=[("vsb", tag)])
        P.emit("s", lambda e: e.activation(v_sb.ap(), v_sb.ap(), AF.Ln,
                                           bias=consts.ap()[0:1, C_EPS:C_EPS + 1]),
               reads=[("vsb", tag), ("consts",)], writes=[("vsb", tag)])
        P.emit("s", lambda e: e.activation(v_sb.ap(), v_sb.ap(), AF.Exp, scale=-0.5),
               reads=[("vsb", tag)], writes=[("vsb", tag)])
        for h in range(2):
            P.emit("t", lambda e, h=h: e.matmul(
                psa[:, (bc_b0 + h) * 512:(bc_b0 + h + 1) * 512], onesrow.ap(),
                m_sb.ap()[:, h * 512:(h + 1) * 512], start=True, stop=True),
                reads=[("msb", tag), ("onesrow",)], writes=bkeys(bc_b0 + h))
            P.emit("t", lambda e, h=h: e.matmul(
                psa[:, (bc_b0 + 2 + h) * 512:(bc_b0 + 3 + h) * 512], onesrow.ap(),
                v_sb.ap()[:, h * 512:(h + 1) * 512], start=True, stop=True),
                reads=[("vsb", tag), ("onesrow",)], writes=bkeys(bc_b0 + 2 + h))
        P.emit("s", lambda e: e.activation(m_b.ap(), psa[:, bc_b0 * 512:(bc_b0 + 2) * 512], AF.Identity),
               reads=bkeys(bc_b0, 2) + [FM], writes=[("mb", tag)])
        P.emit("s", lambda e: e.activation(r_b.ap(), psa[:, (bc_b0 + 2) * 512:(bc_b0 + 4) * 512], AF.Identity),
               reads=bkeys(bc_b0 + 2, 2) + [FM], writes=[("rb", tag)])
        SB.free_("m_sb" + tag); SB.free_("v_sb" + tag); SB.free_("msq" + tag)
        P.merge_into(FM, {"msb", "vsb", "msq"})

    for k in range(16):
        ln_stats_chunk(k, k == 0, k == 15, 0, 2, par=k % 2)
    m_b1 = SB.alloc("m_b1", [128, NT], BF16)
    r_b1 = SB.alloc("r_b1", [128, NT], BF16)
    ln_finish("ln1", m_b1, r_b1, 0, 2, 4)

    ln1 = SB.alloc("ln1", [128, 16 * NT], BF16)
    t_st = SB.alloc("t_st", [128, NT], BF16)
    ln1_ap = ln1.ap().rearrange("p (k n) -> p k n", n=NT)
    t_st2 = SB.alloc("t_st2", [128, NT], BF16)
    for k in range(16):
        ts_ = t_st if k % 2 == 0 else t_st2
        tk = ("tst",) if k % 2 == 0 else ("tst2",)
        P.emit("v", lambda e, k=k, ts_=ts_: e.tensor_sub(
            ts_.ap(), xb16.ap()[:, k * NT:(k + 1) * NT], m_b1.ap()),
            reads=[("xbst", k), ("mb", "ln1"), FM], writes=[tk])
        P.emit("v", lambda e, k=k, ts_=ts_: e.tensor_mul(
            ln1.ap()[:, k * NT:(k + 1) * NT], ts_.ap(), r_b1.ap()),
            reads=[tk, ("rb", "ln1")], writes=[("ln1", k)])
    SB.free_("t_st2")
    SB.free_("xb16"); SB.free_("m_b1"); SB.free_("r_b1"); SB.free_("x_sb")
    SB.free_("xq_st0"); SB.free_("xq_st1"); SB.free_("t_st")
    P.merge_into(FM, {"xs", "mb", "rb", "xbst", "xqst", "tst", "tst2", "msb", "vsb", "msq"})

    def pi_dst(tensor_ap, h, nblk=NBLK):
        return tensor_ap.rearrange("p (b w) -> p b w", w=128)[:, 0:nblk, h * 8:h * 8 + 8]

    def emit_chain(out2banks, lhs_fn, rhs_fn, nk, reads, writes, chunk_keys=None):
        if chunk_keys is None:
            fns = []
            for h in range(2):
                for k in range(nk):
                    fns.append(lambda e, k=k, h=h, st=(k == 0), sp=(k == nk - 1): e.matmul(
                        out2banks[:, h * 512:(h + 1) * 512], lhs_fn(k),
                        rhs_fn(k)[:, h * 512:(h + 1) * 512], start=st, stop=sp))
            P.emit("t", fns, reads=reads, writes=writes)
            return
        for h in range(2):
            for k in range(nk):
                P.emit("t", lambda e, k=k, h=h, st=(k == 0), sp=(k == nk - 1): e.matmul(
                    out2banks[:, h * 512:(h + 1) * 512], lhs_fn(k),
                    rhs_fn(k)[:, h * 512:(h + 1) * 512], start=st, stop=sp),
                    reads=list(reads) + [chunk_keys(k)], writes=writes)

    cq_sb = SB.alloc("cq_sb", [128, 4 * NT], BF16)
    ckv_sb = SB.alloc("ckv_sb", [128, 4 * NT], BF16)
    krr = SB.alloc("krr", [73, 128 * NBLK], BF16)
    srcv = maskuv.ap()[:, None, 128:256].broadcast_to([9, NBLK, 128])
    P.emit("s", lambda e, d=krr.ap()[64:73, :].rearrange("p (b z) -> p b z", z=128), s=srcv:
           e.activation(d, s, AF.Identity),
           reads=[("maskuv",), FM], writes=[("krrm",)])
    rp_a = SB.alloc("rp_a", [128, NT], F32)
    rp_b = SB.alloc("rp_b", [128, NT], F32)
    wb_st = [SB.alloc(f"wb_st{i}", [128, 16 * 128], BF16) for i in range(3)]

    ln1_reads = [("ln1", k) for k in range(16)]

    def rope_emit(j, pre_b0, sw_b0, bias_pre, bias_sw, dst_tensor, dst_tag):
        P.emit("v", lambda e, bp=bias_pre, b0=pre_b0: e.scalar_tensor_tensor(
            rp_a.ap(), bank(b0, 2), ccol(bp), ropeC.ap(), op0=ALU.add, op1=ALU.mult),
            reads=bkeys(pre_b0, 2) + [("consts",), ("ropeC",), FM], writes=[("rpa",)])
        P.emit("v", lambda e, bs=bias_sw, b0=sw_b0: e.scalar_tensor_tensor(
            rp_b.ap(), bank(b0, 2), ccol(bs), ropeS2.ap(), op0=ALU.add, op1=ALU.mult),
            reads=bkeys(sw_b0, 2) + [("consts",), ("ropeS2",), FM], writes=[("rpb",)])
        for hh in range(2):
            head = 2 * j + hh
            dst = pi_dst(dst_tensor.ap()[0:64, :], head)
            src_a = rp_a.ap()[hh * 64:(hh + 1) * 64, :].rearrange("p (b z) -> p b z", z=8)
            src_b = rp_b.ap()[hh * 64:(hh + 1) * 64, :].rearrange("p (b z) -> p b z", z=8)
            P.emit("v", lambda e, d=dst, a=src_a, b=src_b: e.tensor_add(d, a, b),
                   reads=[("rpa",), ("rpb",), FM], writes=[(dst_tag, head)])

    b_order = list(range(8)) + [m for j in range(8) for m in (8 + j, 16 + j)]
    for idx, m in enumerate(b_order):
        sl = idx % 3
        wslice_dma(hb["WB"], 16, 128, m, wb_st, sl, "wbc")
        if m < 8:
            b0 = (idx % 2) * 2
        elif m < 16:
            b0 = 4
        else:
            b0 = 6
        emit_chain(bank(b0, 2), lambda k, sl=sl: wb_st[sl].ap()[:, k * 128:(k + 1) * 128],
                   lambda k: ln1_ap[:, k, :], 16,
                   reads=[("wbc", sl)] + ([] if idx < 2 else ln1_reads), writes=bkeys(b0, 2),
                   chunk_keys=(lambda k: ("ln1", k)) if idx < 2 else None)
        if m < 4:
            P.emit("s", lambda e, m=m, b0=b0: e.activation(
                cq_sb.ap()[:, m * NT:(m + 1) * NT], bank(b0, 2), AF.Identity, bias=ccol(C_BC + m)),
                reads=bkeys(b0, 2) + [("consts",), FM], writes=[("cq", m)])
        elif m < 8:
            P.emit("s", lambda e, m=m, b0=b0: e.activation(
                ckv_sb.ap()[:, (m - 4) * NT:(m - 3) * NT], bank(b0, 2), AF.Identity,
                bias=ccol(C_BCKV + m - 4)),
                reads=bkeys(b0, 2) + [("consts",), FM], writes=[("ckv", m - 4)])
        elif m >= 16:
            j = m - 16
            rope_emit(j, 4, 6, C_BKR + j, C_BKRSW + j, krr, "krr")

    tap("cq", cq_sb.ap(), [128, 4 * NT], BF16, [("cq", c) for c in range(4)])
    tap("ckv", ckv_sb.ap(), [128, 4 * NT], BF16, [("ckv", c) for c in range(4)])
    tap("krr", krr.ap(), [64, 128 * NBLK], BF16, [("krr", h) for h in range(16)])
    SB.free_("ln1")
    for i in range(3):
        SB.free_(f"wb_st{i}")
    P.merge_into(FM, {"ln1", "wbc"})

    kn = SB.alloc("kn", [128, 128 * NBLK], BF16)
    vnat = SB.alloc("vnat", [128, 128 * NBLK], BF16)
    vt_st = SB.alloc("vt_st", [128, 128 * NBLK], BF16)
    wd_st = [SB.alloc(f"wd_st{i}", [128, 4 * 512], BF16) for i in range(2)]

    def ckv_rhs(k):
        return ckv_sb.ap()[:, k * NT:(k + 1) * NT]

    ckv_reads = [("ckv", c) for c in range(4)]
    for h in range(16):
        sl = (h // 4) % 2
        if h % 4 == 0:
            wslice_dma(hb["WD"], 4, 512, h // 4, wd_st, sl, "wdc")
        sub = h % 4
        b0 = (h % 3) * 2
        emit_chain(bank(b0, 2),
                   lambda k, sl=sl, sub=sub: wd_st[sl].ap()[:, k * 512 + sub * 128:k * 512 + (sub + 1) * 128],
                   ckv_rhs, 4, reads=[("wdc", sl)] + ckv_reads, writes=bkeys(b0, 2))
        dst = pi_dst(kn.ap(), h)
        P.emit("s", lambda e, h=h, b0=b0, d=dst: e.activation(
            d, bank(b0, 2).rearrange("p (b z) -> p b z", z=8), AF.Identity, bias=ccol(C_BCK + h)),
            reads=bkeys(b0, 2) + [("consts",), FM], writes=[("kn", h)])
    for h in range(16):
        sl = (h // 4) % 2
        if h % 4 == 0:
            wslice_dma(hb["WD"], 4, 512, 4 + h // 4, wd_st, sl, "wdc")
        sub = h % 4
        b0 = 4 + (h % 2) * 2
        emit_chain(bank(b0, 2),
                   lambda k, sl=sl, sub=sub: wd_st[sl].ap()[:, k * 512 + sub * 128:k * 512 + (sub + 1) * 128],
                   ckv_rhs, 4, reads=[("wdc", sl)] + ckv_reads, writes=bkeys(b0, 2))
        dst = pi_dst(vt_st.ap(), h)
        P.emit("s", lambda e, h=h, b0=b0, d=dst: e.activation(
            d, bank(b0, 2).rearrange("p (b z) -> p b z", z=8), AF.Identity, bias=ccol(C_BV + h)),
            reads=bkeys(b0, 2) + [("consts",), FM], writes=[("vts", h)])
    vts_reads = [("vts", h) for h in range(16)]
    for grp in range(NBLK // 8):
        pb = grp % 2
        fns = []
        for j in range(8):
            bb = grp * 8 + j
            fns.append(lambda e, bb=bb, j=j, pb=pb: e.transpose(
                bank(pb, 1, BF16)[:, j * 128:(j + 1) * 128],
                vt_st.ap()[:, bb * 128:(bb + 1) * 128], ident.ap()))
        P.emit("t", fns, reads=vts_reads + [("ident",)], writes=bkeys(pb))
        P.emit("s", lambda e, grp=grp, pb=pb: e.activation(
            vnat.ap()[:, grp * 1024:(grp + 1) * 1024], bank(pb, 1, BF16), AF.Identity),
            reads=bkeys(pb) + [FM], writes=[("vnat8", grp)])
    tap("kn", kn.ap(), [128, 128 * NBLK], BF16, [("kn", h) for h in range(16)])
    tap("vnat", vnat.ap(), [128, 128 * NBLK], BF16, [("vnat", g) for g in range(NBLK)])
    SB.free_("vt_st"); SB.free_("ckv_sb")
    P.merge_into(FM, {"vts", "ckv", "wdc"})
    for i in range(2):
        SB.free_(f"wd_st{i}")

    qn = SB.alloc("qn", [128, 128 * NBLK], BF16)
    qrr = SB.alloc("qrr", [73, 128 * NBLK], BF16)
    srcu = maskuv.ap()[:, None, 0:128].broadcast_to([9, NBLK, 128])
    P.emit("s", lambda e, d=qrr.ap()[64:73, :].rearrange("p (b z) -> p b z", z=128), s=srcu:
           e.activation(d, s, AF.Identity),
           reads=[("maskuv",), FM], writes=[("qrrm",)])
    wc_st = [SB.alloc(f"wc_st{i}", [128, 4 * 512], BF16) for i in range(2)]

    def cq_rhs(k):
        return cq_sb.ap()[:, k * NT:(k + 1) * NT]

    cq_reads = [("cq", c) for c in range(4)]
    c_order = list(range(16)) + [m for j in range(8) for m in (16 + j, 24 + j)]
    wc_slot = {0: 0, 1: 1, 2: 0, 3: 1, 4: 0, 5: 0, 6: 1, 7: 1}
    wc_loaded = set()
    for idx, m in enumerate(c_order):
        slj = m // 4
        sl = wc_slot[slj]
        if slj not in wc_loaded:
            wc_loaded.add(slj)
            wslice_dma(hb["WC"], 4, 512, slj, wc_st, sl, "wcc")
        sub = m % 4
        if m < 16:
            b0 = (idx % 3) * 2
        elif m < 24:
            b0 = 4
        else:
            b0 = 6
        emit_chain(bank(b0, 2),
                   lambda k, sl=sl, sub=sub: wc_st[sl].ap()[:, k * 512 + sub * 128:k * 512 + (sub + 1) * 128],
                   cq_rhs, 4, reads=[("wcc", sl)] + cq_reads, writes=bkeys(b0, 2))
        if m < 16:
            dst = pi_dst(qn.ap(), m)
            P.emit("s", lambda e, m=m, b0=b0, d=dst: e.activation(
                d, bank(b0, 2).rearrange("p (b z) -> p b z", z=8), AF.Identity, bias=ccol(C_BCQ + m)),
                reads=bkeys(b0, 2) + [("consts",), FM], writes=[("qn", m)])
        elif m >= 24:
            j = m - 24
            rope_emit(j, 4, 6, C_BQR + j, C_BQRSW + j, qrr, "qrr")

    tap("qn", qn.ap(), [128, 128 * NBLK], BF16, [("qn", h) for h in range(16)])
    tap("qrr", qrr.ap(), [64, 128 * NBLK], BF16, [("qrr", h) for h in range(16)])
    SB.free_("cq_sb"); SB.free_("rp_a"); SB.free_("rp_b")
    SB.free_("ropeC"); SB.free_("ropeS2")
    for i in range(2):
        SB.free_(f"wc_st{i}")
    P.merge_into(FM, {"cq", "rpa", "rpb", "ropeC", "ropeS2", "wcc"})

    woin = SB.alloc("woin", [128, 16 * NT], BF16)
    smx = SB.alloc("smx", [128, 6144], BF16)
    _sap = smx.ap()

    def carve(off_b, cols, dtype):
        n = cols * mybir.dt.size(dtype) // 2
        return _sap[:, off_b // 2: off_b // 2 + n].bitcast(dtype)

    e_bf = [carve(i * 2048, 1024, BF16) for i in range(2)]
    attn_st = [carve(4096 + i * 2048, 1024, BF16) for i in range(2)]
    attnT_st = [carve(8192 + i * 2048, 1024, BF16) for i in range(2)]
    mx_sb = SB.alloc("mx_sb", [128, 32], F32)
    mx_st = [mx_sb.ap()[:, i * 16:(i + 1) * 16] for i in range(2)]

    qn_reads = [("qn", h) for h in range(16)]
    kn_reads = [("kn", h) for h in range(16)]
    qr_reads = [("qrr", h) for h in range(16)] + [("krr", h) for h in range(16)]

    def emit_scores(g):
        sb0 = (g % 2) * 2
        fns = []
        for j in range(8):
            blk = g * 8 + j
            col = slice(blk * 128, (blk + 1) * 128)
            fns.append(lambda e, col=col, j=j, sb0=sb0: e.matmul(
                bank(sb0, 2)[:, j * 128:(j + 1) * 128], qn.ap()[:, col], kn.ap()[:, col],
                start=True, stop=False))
            fns.append(lambda e, col=col, j=j, sb0=sb0: e.matmul(
                bank(sb0, 2)[:, j * 128:(j + 1) * 128], qrr.ap()[0:73, col], krr.ap()[0:73, col],
                start=False, stop=True))
        P.emit("t", fns, reads=qn_reads + kn_reads + qr_reads + [("qrrm",), ("krrm",), FM],
               writes=bkeys(sb0, 2))

    def emit_subexp(g):
        p = g % 2
        sb0 = p * 2
        sc3 = bank(sb0, 2).rearrange("p (b z) -> p b z", z=128)
        P.emit("v", lambda e, sc3=sc3, p=p: e.reduce_max(
            mx_st[p][:, 0:8].rearrange("p (b z) -> p b z", z=1), sc3, axis=AX.X),
            reads=bkeys(sb0, 2), writes=[("mx", p)])
        mxb = mx_st[p][:, 0:8][:, :, None].broadcast_to([128, 8, 128])
        P.emit("v", lambda e, sc3=sc3, mxb=mxb, p=p: e.tensor_sub(
            e_bf[p].rearrange("p (b z) -> p b z", z=128), sc3, mxb),
            reads=bkeys(sb0, 2) + [("mx", p), FM], writes=[("ebf", p)])
        P.emit("s", lambda e, p=p: e.activation(e_bf[p], e_bf[p], AF.Exp, scale=SCALER),
               reads=[("ebf", p), FM], writes=[("ebf", p)])

    def emit_soft(g):
        p = g % 2
        P.emit("v", lambda e, p=p: e.reduce_sum(
            mx_st[p][:, 8:16].rearrange("p (b z) -> p b z", z=1),
            e_bf[p].rearrange("p (b z) -> p b z", z=128), axis=AX.X),
            reads=[("ebf", p)], writes=[("sums", p)])
        P.emit("v", lambda e, p=p: e.reciprocal(mx_st[p][:, 8:16], mx_st[p][:, 8:16]),
               reads=[("sums", p)], writes=[("sums", p)])
        rsb = mx_st[p][:, 8:16][:, :, None].broadcast_to([128, 8, 128])
        P.emit("v", lambda e, rsb=rsb, p=p: e.tensor_mul(
            attn_st[p].rearrange("p (b z) -> p b z", z=128),
            e_bf[p].rearrange("p (b z) -> p b z", z=128), rsb),
            reads=[("ebf", p), ("sums", p), FM], writes=[("attn", p)])

    def emit_av(g):
        p = g % 2
        tb = 4 if p == 0 else 7
        fns = []
        for j in range(8):
            fns.append(lambda e, j=j, p=p, tb=tb: e.transpose(
                bank(tb, 1, BF16)[:, j * 128:(j + 1) * 128],
                attn_st[p][:, j * 128:(j + 1) * 128], ident.ap()))
        P.emit("t", fns, reads=[("attn", p), ("ident",)], writes=bkeys(tb))
        P.emit("s", lambda e, p=p, tb=tb: e.activation(attnT_st[p], bank(tb, 1, BF16), AF.Identity),
               reads=bkeys(tb) + [FM], writes=[("attnT", p)])
        fns = []
        for j in range(8):
            blk = g * 8 + j
            fns.append(lambda e, j=j, blk=blk, p=p: e.matmul(
                bank(5, 2)[:, j * 128:(j + 1) * 128], vnat.ap()[:, blk * 128:(blk + 1) * 128],
                attnT_st[p][:, j * 128:(j + 1) * 128], start=True, stop=True))
        P.emit("t", fns, reads=[("attnT", p), ("vnat8", g)], writes=bkeys(5, 2))
        dstw = woin.ap().rearrange("p (i t) -> p i t", t=NT)[:, :, g * 64:(g + 1) * 64] \
                   .rearrange("p i (j z) -> p j i z", z=8)
        srcp = bank(5, 2).rearrange("p (j i z) -> p j i z", i=16, z=8)
        P.emit("s", lambda e, d=dstw, s=srcp: e.activation(d, s, AF.Identity),
            reads=bkeys(5, 2) + [FM], writes=[("woing", g)])

    emit_scores(0)
    emit_subexp(0)
    if True:
        emit_scores(1)
    for g in range(16):
        if g + 1 < 16:
            emit_subexp(g + 1)
        if g + 2 < 16:
            emit_scores(g + 2)
        emit_soft(g)
        emit_av(g)

    for nm in ("qn", "kn", "qrr", "krr", "vnat"):
        SB.free_(nm)
    SB.free_("smx"); SB.free_("mx_sb")
    P.merge_into(FM, {"qn", "kn", "qrr", "krr", "vnat8", "scst", "ebf", "mest", "attn", "attnT", "mx", "sums"})

    tap("woin", woin.ap(), [128, 16 * NT], BF16, [("woin", h) for h in range(16)])
    wo_st = [SB.alloc(f"wo_st{i}", [128, 16 * 128], BF16) for i in range(3)]
    xch_st = [SB.alloc(f"xch_st{i}", [128, NT], F32) for i in range(2)]
    x2_st = [SB.alloc(f"x2_st{i}", [128, NT], F32) for i in range(2)]
    x2b16 = SB.alloc("x2b16", [128, 16 * NT], BF16)
    xq2_st = [SB.alloc(f"xq2_st{i}", [128, NT], BF16) for i in range(2)]
    woin_reads = [("woing", g) for g in range(16)]

    x2s_r = hb["x2s"].rearrange("(m p) n -> p m n", p=128)
    xT_r = hb["xT"].rearrange("(m p) n -> p m n", p=128)
    for m in range(16):
        sl = m % 2
        wsl = m % 3
        wslice_dma(hb["WO"], 16, 128, m, wo_st, wsl, "wos")
        P.emit("y", lambda e, m=m, sl=sl: e.dma_start(out=xch_st[sl].ap(), in_=xT_r[:, m, :]),
               reads=[FM], writes=[("xch", sl)], dma=True)
        b0 = (m % 2) * 2
        fns = []
        for hh in range(2):
            for k in range(16):
                fns.append(lambda e, k=k, hh=hh, wsl=wsl, b0=b0, st=(k == 0): e.matmul(
                    bank(b0, 2)[:, hh * 512:(hh + 1) * 512],
                    wo_st[wsl].ap()[:, k * 128:(k + 1) * 128],
                    woin.ap()[:, k * NT + hh * 512: k * NT + (hh + 1) * 512],
                    start=st, stop=(k == 15)))
        P.emit("t", fns, reads=[("wos", wsl)] + woin_reads, writes=bkeys(b0, 2))
        P.emit("v", lambda e, m=m, sl=sl, b0=b0: e.scalar_tensor_tensor(
            x2_st[sl].ap(), bank(b0, 2), ccol(C_BO + m), xch_st[sl].ap(),
            op0=ALU.add, op1=ALU.add),
            reads=bkeys(b0, 2) + [("xch", sl), ("consts",), FM], writes=[("x2st", sl)])
        P.emit("y", lambda e, m=m, sl=sl: e.dma_start(out=x2s_r[:, m, :], in_=x2_st[sl].ap()),
               reads=[("x2st", sl)], writes=[("x2h", m)], dma=True)
        P.emit("s", lambda e, m=m, sl=sl: e.activation(
            x2b16.ap()[:, m * NT:(m + 1) * NT], x2_st[sl].ap(), AF.Identity),
            reads=[("x2st", sl), FM], writes=[("xbst", m)])

    for k in range(16):
        par = k % 2
        P.emit("s", lambda e, k=k, par=par: e.activation(
            xq2_st[par].ap(), x2b16.ap()[:, k * NT:(k + 1) * NT], AF.Square),
            reads=[("xbst", k)], writes=[("xqst", par)])
        for h in range(2):
            P.emit("t", lambda e, h=h, k=k, st=(k == 0): e.matmul(
                psa[0:1, 2048 + h * 512:2048 + (h + 1) * 512], onesbf.ap(),
                x2b16.ap()[:, k * NT + h * 512:k * NT + (h + 1) * 512], start=st, stop=(k == 15)),
                reads=[("xbst", k), ("onesbf",)], writes=bkeys(4 + h))
            P.emit("t", lambda e, h=h, par=par, st=(k == 0): e.matmul(
                psa[0:1, 3072 + h * 512:3072 + (h + 1) * 512], onesbf.ap(),
                xq2_st[par].ap()[:, h * 512:(h + 1) * 512], start=st, stop=(k == 15)),
                reads=[("xqst", par), ("onesbf",)], writes=bkeys(6 + h))
    for nm in ("woin", "wo_st0", "wo_st1", "wo_st2", "xq2_st0", "xq2_st1", "x2_st0", "x2_st1"):
        SB.free_(nm)
    P.merge_into(FM, {"woin", "woing", "wos", "xqst", "x2st"})

    if "x2" in taps:
        dbg_x2 = nc.dram_tensor("dbg_x2", [D, NT], F32, kind="ExternalOutput").ap()
        P.emit("y", lambda e: e.dma_start(out=dbg_x2, in_=hb["x2s"]),
               reads=[("x2h", m) for m in range(16)], writes=[("tapk", "x2")], dma=True)
        tap_keys.append(("tapk", "x2"))

    m_b2 = SB.alloc("m_b2", [128, NT], BF16)
    r_b2 = SB.alloc("r_b2", [128, NT], BF16)
    ln_finish("ln2", m_b2, r_b2, 4, 6, 0)

    ln2 = SB.alloc("ln2", [128, 16 * NT], BF16)
    t2_st = SB.alloc("t2_st", [128, NT], BF16)
    ln2_ap = ln2.ap().rearrange("p (k n) -> p k n", n=NT)
    t2b_st = SB.alloc("t2b_st", [128, NT], BF16)
    for k in range(16):
        ts_ = t2_st if k % 2 == 0 else t2b_st
        tk = ("tst",) if k % 2 == 0 else ("tst2",)
        P.emit("v", lambda e, k=k, ts_=ts_: e.tensor_sub(
            ts_.ap(), x2b16.ap()[:, k * NT:(k + 1) * NT], m_b2.ap()),
            reads=[("xbst", k), ("mb", "ln2"), FM], writes=[tk])
        P.emit("v", lambda e, k=k, ts_=ts_: e.tensor_mul(
            ln2.ap()[:, k * NT:(k + 1) * NT], ts_.ap(), r_b2.ap()),
            reads=[tk, ("rb", "ln2")], writes=[("ln2", k)])
    SB.free_("t2b_st"); SB.free_("x2b16")
    tap("ln2", ln2.ap(), [128, 16 * NT], BF16, [("ln2", k) for k in range(16)])
    tap("mb2", m_b2.ap(), [128, NT], F32, [("mb", "ln2")])
    SB.free_("m_b2"); SB.free_("r_b2"); SB.free_("t2_st")
    P.merge_into(FM, {"mb", "rb", "tst", "tst2", "xbst", "msb", "vsb", "msq"})

    gated4 = [SB.alloc(f"gated{i}", [128, 16 * NT], BF16) for i in range(4)]
    wgu_st = [SB.alloc(f"wgu_st{i}", [128, 2 * 16 * 128], BF16) for i in range(2)]

    def gated_ap(c):
        return gated4[c // 16].ap()[:, (c % 16) * NT:(c % 16 + 1) * NT]
    sg_st = SB.alloc("sg_st", [128, NT], F32)
    ln2_reads = [("ln2", k) for k in range(16)]
    wdn_st = [SB.alloc("wdn_st0", [128, 64 * 128], BF16), None]
    wslice_dma(hb["WDN"], 64, 128, 0, wdn_st, 0, "wdn")
    for c in range(64):
        sl = c % 2
        wslice_dma(hb["WGU"], 32, 128, c, wgu_st, sl, "wgu")
        bg = (c % 2) * 4
        bu = bg + 2
        for half_w, b0 in ((0, bg), (1, bu)):
            fns = []
            for hh in range(2):
                for k in range(16):
                    fns.append(lambda e, k=k, hh=hh, sl=sl, b0=b0, hw=half_w, st=(k == 0): e.matmul(
                        bank(b0, 2)[:, hh * 512:(hh + 1) * 512],
                        wgu_st[sl].ap()[:, hw * 2048 + k * 128:hw * 2048 + (k + 1) * 128],
                        ln2_ap[:, k, hh * 512:(hh + 1) * 512], start=st, stop=(k == 15)))
            if c < 2:
                for hh in range(2):
                    for k in range(16):
                        P.emit("t", lambda e, k=k, hh=hh, sl=sl, b0=b0, hw=half_w, st=(k == 0): e.matmul(
                            bank(b0, 2)[:, hh * 512:(hh + 1) * 512],
                            wgu_st[sl].ap()[:, hw * 2048 + k * 128:hw * 2048 + (k + 1) * 128],
                            ln2_ap[:, k, hh * 512:(hh + 1) * 512], start=st, stop=(k == 15)),
                            reads=[("wgu", sl), ("ln2", k)], writes=bkeys(b0, 2))
            else:
                P.emit("t", fns, reads=[("wgu", sl)] + ln2_reads, writes=bkeys(b0, 2))
        P.emit("s", lambda e, bg=bg: e.activation(sg_st.ap(), bank(bg, 2), AF.Silu),
               reads=bkeys(bg, 2), writes=[("sg",)])
        P.emit("v", lambda e, c=c, bu=bu: e.tensor_mul(
            gated_ap(c), sg_st.ap(), bank(bu, 2)),
            reads=[("sg",), FM] + bkeys(bu, 2), writes=[("gated", c)])

    for gi in range(4):
        tap(f"gated{gi}", gated4[gi].ap(), [128, 16 * NT], BF16,
            [("gated", c) for c in range(gi * 16, gi * 16 + 16)])
    SB.free_("ln2")
    for nm in ("wgu_st0", "wgu_st1", "sg_st"):
        SB.free_(nm)
    P.merge_into(FM, {"ln2", "wg", "wu", "sg"})

    wdn_st[1] = SB.alloc("wdn_st1", [128, 64 * 128], BF16)
    out_st = [SB.alloc(f"out_st{i}", [128, NT], F32) for i in range(2)]
    outT_r = hb["outT"].rearrange("(m p) n -> p m n", p=128)
    gated_reads = [("gated", c) for c in range(64)]
    for m in range(16):
        sl = m % 2
        wsl = m % 2
        if m > 0:
            wslice_dma(hb["WDN"], 64, 128, m, wdn_st, wsl, "wdn")
        P.emit("y", lambda e, m=m, sl=sl: e.dma_start(out=xch_st[sl].ap(), in_=x2s_r[:, m, :]),
               reads=[("x2h", m), FM], writes=[("xch", sl)], dma=True)
        b0 = (m % 2) * 2
        fns = []
        for hh in range(2):
            for k in range(64):
                fns.append(lambda e, k=k, hh=hh, wsl=wsl, b0=b0, st=(k == 0): e.matmul(
                    bank(b0, 2)[:, hh * 512:(hh + 1) * 512],
                    wdn_st[wsl].ap()[:, k * 128:(k + 1) * 128],
                    gated_ap(k)[:, hh * 512:(hh + 1) * 512],
                    start=st, stop=(k == 63)))
        P.emit("t", fns, reads=[("wdn", wsl)] + gated_reads, writes=bkeys(b0, 2))
        P.emit("v", lambda e, sl=sl, b0=b0: e.tensor_add(
            out_st[sl].ap(), bank(b0, 2), xch_st[sl].ap()),
            reads=bkeys(b0, 2) + [("xch", sl), FM], writes=[("outst", sl)])
        P.emit("y", lambda e, m=m, sl=sl: e.dma_start(out=outT_r[:, m, :], in_=out_st[sl].ap()),
               reads=[("outst", sl)], writes=[("outh", m)], dma=True)

    P.emit("y", lambda e: e.nop(), reads=[("outh", m) for m in range(16)] + tap_keys)

    P.replay()
    P.stack.close()
    return nc


def _rope_tables(core):
    pos0 = (core % 2) * NT
    t = np.arange(pos0, pos0 + NT, dtype=np.float64)
    inv = 1.0 / (10000.0 ** (np.arange(0, DRH, 2, dtype=np.float64) / DRH))
    ang = t[None, :] * inv[:, None]
    cos64 = np.concatenate([np.cos(ang), np.cos(ang)], axis=0)
    s2_64 = np.concatenate([-np.sin(ang), np.sin(ang)], axis=0)
    cosT = np.concatenate([cos64, cos64], axis=0).astype(np.float32)
    s2T = np.concatenate([s2_64, s2_64], axis=0).astype(np.float32)
    return np.ascontiguousarray(cosT), np.ascontiguousarray(s2T)


def _swap_cols(w):
    w = w.reshape(w.shape[0], -1, 64)
    return np.ascontiguousarray(
        np.concatenate([w[:, :, 32:], w[:, :, :32]], axis=2).reshape(w.shape[0], -1))


def _swap_bias(b):
    b = b.reshape(-1, 64)
    return np.ascontiguousarray(np.concatenate([b[:, 32:], b[:, :32]], axis=1).reshape(-1))


def _prep_shared(i):
    f32 = lambda a: np.ascontiguousarray(np.asarray(a, dtype=np.float32))
    Wc, Wckv = f32(i["L0_attn_Wc"]), f32(i["L0_attn_Wckv"])
    Wkr = f32(i["L0_attn_Wkr"])
    Wcq, Wqr = f32(i["L0_attn_Wcq"]), f32(i["L0_attn_Wqr"])
    Wck, Wv = f32(i["L0_attn_Wck"]), f32(i["L0_attn_Wv"])
    def repack(W, cw=128):
        K, N = W.shape
        nk, nm = K // 128, N // cw
        return np.ascontiguousarray(
            W.reshape(nk, 128, nm, cw).transpose(2, 1, 0, 3).reshape(nm, 128, nk * cw))

    sh = {}
    sh["WB"] = repack(np.concatenate([Wc, Wckv, Wkr, _swap_cols(Wkr)], axis=1))
    sh["WC"] = repack(np.concatenate([Wcq, Wqr, _swap_cols(Wqr)], axis=1), cw=512)
    sh["WD"] = repack(np.concatenate([Wck, Wv], axis=1), cw=512)
    sh["WO"] = repack(f32(i["L0_attn_Wo"]))
    wg_r = repack(np.ascontiguousarray(f32(i["L0_ffn_w_gate"]).T))
    wu_r = repack(np.ascontiguousarray(f32(i["L0_ffn_w_up"]).T))
    sh["WGU"] = np.ascontiguousarray(np.concatenate([wg_r, wu_r], axis=2))
    sh["WDN"] = repack(np.ascontiguousarray(f32(i["L0_ffn_w_down"]).T))
    consts = np.zeros((128, NCONST), dtype=np.float32)
    consts[:, C_EPS] = EPS
    consts[:, C_ONE] = 1.0

    def put(col0, vec):
        v = np.asarray(vec, dtype=np.float32).reshape(-1, 128)
        for c in range(v.shape[0]):
            consts[:, col0 + c] = v[c]

    put(C_BC, i["L0_attn_bc"]); put(C_BCKV, i["L0_attn_bckv"])
    put(C_BKR, i["L0_attn_bkr"]); put(C_BKRSW, _swap_bias(f32(i["L0_attn_bkr"])))
    put(C_BCQ, i["L0_attn_bcq"])
    put(C_BQR, i["L0_attn_bqr"]); put(C_BQRSW, _swap_bias(f32(i["L0_attn_bqr"])))
    put(C_BCK, i["L0_attn_bck"]); put(C_BV, i["L0_attn_bv"])
    put(C_BO, i["L0_attn_bo"])
    sh["consts"] = consts
    r = np.arange(128)
    uv = np.zeros((9, 256), dtype=np.float32)
    for rr in range(8):
        uv[rr, 0:128] = (r % 8 == rr)
        uv[rr, 128:256] = 416.0 * (r % 8 == rr)
    uv[8, 0:128] = 1.0
    uv[8, 128:256] = -416.0
    sh["maskuv"] = uv
    sh["ident"] = np.eye(128, dtype=np.float32)
    sh["onesrow"] = np.ones((1, 128), dtype=np.float32)
    return sh


def kernel(**inputs):
    x = np.asarray(inputs["L0_attn_h_t"], dtype=np.float32)
    xf = x.reshape(NCORE * NT, D)
    sh = _prep_shared(inputs)
    nc = build_program()
    in_maps = []
    for c in range(NCORE):
        cosT, s2T = _rope_tables(c)
        m = dict(sh)
        m["xT"] = np.ascontiguousarray(xf[c * NT:(c + 1) * NT].T)
        m["ropeC"] = cosT
        m["ropeS2"] = s2T
        in_maps.append(m)
    res = run_bass_kernel_spmd(nc, in_maps, core_ids=list(range(NCORE)))
    out = np.empty((NCORE * NT, D), dtype=np.float32)
    for c in range(NCORE):
        out[c * NT:(c + 1) * NT] = np.asarray(res.results[c]["outT"]).T
    return out.reshape(4, SEQ, D)
